# version 06 (final)
# speedup vs baseline: 1.0409x; 1.0109x over previous
import numpy as np
from contextlib import ExitStack
import concourse.bass as bass
import concourse.mybir as mybir
from concourse.bass_utils import run_bass_kernel_spmd

F32 = mybir.dt.float32
BF16 = mybir.dt.bfloat16
I32 = mybir.dt.int32
U32 = mybir.dt.uint32
AF = mybir.ActivationFunctionType
ALU = mybir.AluOpType
AX = mybir.AxisListType

TPC = 2048
NCORES = 8
DM = 2048
DIN = 4096
NST = 16
DTR = 128
GH = 5504
MOE_H = 2816
MOE_E = 8
MOE_C = 2048
MOE_T = NCORES * TPC


class S:

    def __init__(self, handle):
        self.h = handle
        self.n = 0

    def bump(self, k=1):
        self.n += k
        return self.n

    def total(self):
        return (self, self.n)


class Res:

    def __init__(self):
        self.deps = []

    def set(self, *deps):
        self.deps = [d for d in deps if d is not None]

    def waits(self):
        return list(self.deps)


class B:

    ENGS = ("sync", "tensor", "scalar", "vector", "gpsimd")

    def __init__(self, nc, st):
        self.nc = nc
        self.st = st
        self.steps = []
        self._semid = 0
        self._pool = {}
        self._pending = {}

    GLOBAL_SEMS = ("sPE", "sACT", "sDVE", "sPOOL", "sBAR", "sCC")

    def new_phase(self):
        self._alias = {}

    def sem(self, name=None):
        if name is None:
            self._semid += 1
            name = f"s{self._semid}"
        if name not in self.GLOBAL_SEMS:
            if not hasattr(self, "_alias"):
                self._alias = {}
            name = self._alias.setdefault(name, f"g{len(self._alias)}")
        if name not in self._pool:
            self._pool[name] = S(self.st.enter_context(self.nc.semaphore(name)))
        return self._pool[name]

    def sems(self, k, name=None):
        return [self.sem(f"{name or 'q'}{i}") for i in range(k)]

    def barrier(self, deps):
        self.new_phase()
        sBAR = self.sem("sBAR")
        for e in self.ENGS:
            self.op(e, (lambda eng: eng.nop()), inc=sBAR, waits=deps)
        lvl = [(sBAR, sBAR.n)]
        for e in self.ENGS:
            self._pending[e] = lvl
        return lvl

    def op(self, eng_name, emit_fn, inc=None, inc_k=1, waits=()):
        waits = list(waits) + self._pending.pop(eng_name, [])
        flat = []
        for w in waits:
            if w is None:
                continue
            if isinstance(w, Res):
                flat.extend(w.waits())
            else:
                flat.append(w)
        best = {}
        for (s, v) in flat:
            if v and v > best.get(id(s), (s, 0))[1]:
                best[id(s)] = (s, v)
        vals = tuple(best.values())
        post = inc.bump(inc_k) if inc is not None else None

        def run(eng):
            for (s, v) in vals:
                eng.wait_ge(s.h, v)
            inst = emit_fn(eng)
            if inc is not None:
                inst.then_inc(inc.h, inc_k)

        self.steps.append((eng_name, run))
        return (inc, post) if inc is not None else None

    def finalize(self, block):
        used = []
        for e in ("sync", "tensor", "scalar", "vector", "gpsimd"):
            if any(s[0] == e for s in self.steps):
                used.append(e)

        def make(eng_name):
            def emit(eng):
                for (e, run) in self.steps:
                    if e == eng_name:
                        run(eng)
            return emit

        for e in used:
            getattr(block, e)(make(e))


def setup_consts(nc, csts, b, sPOOL, values):
    deps = []
    for i, val in enumerate(values):
        deps.append(b.op("gpsimd", lambda g, i=i, val=val: g.memset(csts[:, i:i + 1], val), inc=sPOOL))
        nc.const_aps.register(F32, val, csts[:, i:i + 1])
    return deps


def col1(dram_ap_1d):
    return dram_ap_1d.rearrange("(p one) -> p one", one=1)



DT_STAGE = 9


def emit_p1(nc, b, D):
    T = TPC
    TE = T + 3
    NC = 512
    NCH = T // NC
    NT = DIN // 128
    xT = D["xT"]
    w_in = D["w_in"]
    conv_w = D["conv_w"]
    conv_b = D["conv_b"]
    w_xp = D["w_xp"]
    w_dt = D["w_dt"]
    b_dt = D["b_dt"]
    hs_o = D["hs"]
    gate_o = D["gate"]
    dt_o = D["dt"]
    dtu_o = D["dtu"]
    bn_o = D["bn"]
    cn_o = D["cn"]

    st = ExitStack()
    stA = ExitStack()
    with st:
        csts_sb = st.enter_context(nc.sbuf_tensor("csts_sb", [128, 8], F32, allow_name_mangling=True))
        sp0_sb = st.enter_context(nc.sbuf_tensor("sp0_sb", [128, T], F32, allow_name_mangling=True))
        sp1_sb = st.enter_context(nc.sbuf_tensor("sp1_sb", [32, T], F32, allow_name_mangling=True))
        ones1_sb = st.enter_context(nc.sbuf_tensor("ones1_sb", [1, 128], F32, allow_name_mangling=True))
        ones128_sb = st.enter_context(nc.sbuf_tensor("ones128_sb", [128, 1], F32, allow_name_mangling=True))
        sel_sb = st.enter_context(nc.sbuf_tensor("sel_sb", [32, 2], F32, allow_name_mangling=True))
        selbc_sb = st.enter_context(nc.sbuf_tensor("selbc_sb", [2, 32], F32, allow_name_mangling=True))
        io32r_sb = st.enter_context(nc.sbuf_tensor("io32r_sb", [32, 2], F32, allow_name_mangling=True))
        io32c_sb = st.enter_context(nc.sbuf_tensor("io32c_sb", [32, 1], F32, allow_name_mangling=True))
        io2r_sb = st.enter_context(nc.sbuf_tensor("io2r_sb", [2, 32], F32, allow_name_mangling=True))
        io2c_sb = st.enter_context(nc.sbuf_tensor("io2c_sb", [2, 1], F32, allow_name_mangling=True))
        zz_sb = st.enter_context(nc.sbuf_tensor("zz_sb", [32, 4], F32, allow_name_mangling=True))
        x_sb = stA.enter_context(nc.sbuf_tensor("x_sb", [128, 16 * TE], BF16, allow_name_mangling=True))
        wh_sb = stA.enter_context(nc.sbuf_tensor("wh_sb", [128, 2 * 2048], BF16, allow_name_mangling=True))
        wg_sb = stA.enter_context(nc.sbuf_tensor("wg_sb", [128, 2 * 2048], BF16, allow_name_mangling=True))
        wxp_sb = stA.enter_context(nc.sbuf_tensor("wxp_sb", [128, 2 * 160], BF16, allow_name_mangling=True))
        cw_sb = stA.enter_context(nc.sbuf_tensor("cw_sb", [128, 2 * 4], F32, allow_name_mangling=True))
        cb_sb = stA.enter_context(nc.sbuf_tensor("cb_sb", [128, 2], F32, allow_name_mangling=True))
        hid_sb = stA.enter_context(nc.sbuf_tensor("hid_sb", [128, 2 * TE], F32, allow_name_mangling=True))
        cv_sb = stA.enter_context(nc.sbuf_tensor("cv_sb", [128, 2 * T], F32, allow_name_mangling=True))
        hsb_sb = stA.enter_context(nc.sbuf_tensor("hsb_sb", [128, 2 * T], BF16, allow_name_mangling=True))
        gb_sb = stA.enter_context(nc.sbuf_tensor("gb_sb", [128, 2 * T], BF16, allow_name_mangling=True))
        pH = st.enter_context(nc.psum_tensor("pH", [128, NC], F32, allow_name_mangling=True))
        pG = st.enter_context(nc.psum_tensor("pG", [128, NC], F32, allow_name_mangling=True))
        pX = st.enter_context(nc.psum_tensor("pX", [128, NC], F32, allow_name_mangling=True))
        pX2 = st.enter_context(nc.psum_tensor("pX2", [32, NC], F32, allow_name_mangling=True))
        pB = st.enter_context(nc.psum_tensor("pB", [128, 8], F32, allow_name_mangling=True))
        pR = st.enter_context(nc.psum_tensor("pR", [128, NC], F32, allow_name_mangling=True))

        sPE, sACT, sDVE, sPOOL = b.sem("sPE"), b.sem("sACT"), b.sem("sDVE"), b.sem("sPOOL")
        sX = b.sem("sX")
        sWH = b.sems(2, "sWH"); sWG = b.sems(2, "sWG"); sWXP = b.sems(2, "sWXP")
        sCW = b.sems(2, "sCW"); sCB = b.sems(2, "sCB"); sDB = b.sems(2, "sDB")
        sWDT = b.sems(2, "sWDT"); sHSF = b.sems(2, "sHSF")
        sGST = b.sems(2, "sGST"); sHST = b.sems(2, "sHST")
        sDST = b.sems(2, "sDST"); sUST = b.sems(2, "sUST")
        sMISC = b.sem("sMISC")


        r_pH = Res(); r_pG = Res(); r_pX = Res(); r_pX2 = Res(); r_pB = Res(); r_pR = Res()
        r_wh = [Res(), Res()]; r_wg = [Res(), Res()]; r_wxp = [Res(), Res()]
        r_cw = [Res(), Res()]; r_cb = [Res(), Res()]; r_db = [Res(), Res()]
        r_wdt = [Res(), Res()]; r_hsf = [Res(), Res()]
        r_hid = [Res(), Res()]; r_cv = [Res(), Res()]; r_hsb = [Res(), Res()]; r_gb = [Res(), Res()]
        r_dt = [Res(), Res()]; r_dtu = [Res(), Res()]
        r_z1 = [Res(), Res()]; r_z2 = [Res(), Res()]

        v_ones1 = b.op("gpsimd", lambda g: g.memset(ones1_sb[:, :], 1.0), inc=sPOOL)
        v_ones128 = b.op("gpsimd", lambda g: g.memset(ones128_sb[:, :], 1.0), inc=sPOOL)
        v_io32r = b.op("gpsimd", lambda g: g.iota(io32r_sb[:, :], pattern=[[1, 2]], channel_multiplier=0, allow_small_or_imprecise_dtypes=True), inc=sPOOL)
        v_io32c = b.op("gpsimd", lambda g: g.iota(io32c_sb[:, :], pattern=[[0, 1]], channel_multiplier=1, allow_small_or_imprecise_dtypes=True), inc=sPOOL)
        v_io2r = b.op("gpsimd", lambda g: g.iota(io2r_sb[:, :], pattern=[[1, 2], [0, 16]], channel_multiplier=0, allow_small_or_imprecise_dtypes=True), inc=sPOOL)
        v_io2c = b.op("gpsimd", lambda g: g.iota(io2c_sb[:, :], pattern=[[0, 1]], channel_multiplier=1, allow_small_or_imprecise_dtypes=True), inc=sPOOL)
        v1 = b.op("vector", lambda v: v.tensor_scalar(zz_sb[:, 0:2], io32r_sb[:, :], 16.0, io32c_sb[:, 0:1], ALU.mult, ALU.subtract),
                  inc=sDVE, waits=[v_io32r, v_io32c])
        v2 = b.op("vector", lambda v: v.tensor_scalar(zz_sb[:, 2:4], zz_sb[:, 0:2], 0.0, None, ALU.is_le),
                  inc=sDVE, waits=[v1])
        v3 = b.op("vector", lambda v: v.tensor_scalar(zz_sb[:, 0:2], zz_sb[:, 0:2], -16.0, None, ALU.is_gt),
                  inc=sDVE, waits=[v2])
        v_sel = b.op("vector", lambda v: v.tensor_mul(sel_sb[:, :], zz_sb[:, 0:2], zz_sb[:, 2:4]),
                     inc=sDVE, waits=[v3])
        v_selbc = b.op("vector", lambda v: v.tensor_scalar(selbc_sb[:, :], io2r_sb[:, :], io2c_sb[:, 0:1], None, ALU.is_equal),
                       inc=sDVE, waits=[v_io2r, v_io2c])

        for k in range(16):
            b.op("gpsimd", lambda g, k=k: g.dma_start(out=x_sb[:, k * TE:(k + 1) * TE], in_=xT[k * 128:(k + 1) * 128, :]),
                 inc=sX, inc_k=16)
        v_xload = sX.total()

        v_sp0 = b.op("gpsimd", lambda g: g.memset(sp0_sb[:, :], 0.0), inc=sPOOL)
        v_sp1 = b.op("gpsimd", lambda g: g.memset(sp1_sb[:, :], 0.0), inc=sPOOL)

        v_xp_last = None
        for c in range(NT):
            par = c & 1
            c0 = c * 128
            for k in range(16):
                wlh = b.op("gpsimd", lambda g, c0=c0, par=par, k=k: g.dma_start(
                    out=wh_sb[:, par * 2048 + k * 128:par * 2048 + (k + 1) * 128],
                    in_=w_in[k * 128:(k + 1) * 128, c0:c0 + 128]),
                    inc=sWH[par], inc_k=16, waits=[r_wh[par]] if k == 0 else [])
            wlh = sWH[par].total()
            for k in range(16):
                wlg = b.op("gpsimd", lambda g, c0=c0, par=par, k=k: g.dma_start(
                    out=wg_sb[:, par * 2048 + k * 128:par * 2048 + (k + 1) * 128],
                    in_=w_in[k * 128:(k + 1) * 128, DIN + c0:DIN + c0 + 128]),
                    inc=sWG[par], inc_k=16, waits=[r_wg[par]] if k == 0 else [])
            wlg = sWG[par].total()
            wlx = b.op("gpsimd", lambda g, c0=c0, par=par: g.dma_start(
                out=wxp_sb[:, par * 160:(par + 1) * 160], in_=w_xp[c0:c0 + 128, :]),
                inc=sWXP[par], inc_k=16, waits=[r_wxp[par]])
            wlc = b.op("sync", lambda s, c0=c0, par=par: s.dma_start(
                out=cw_sb[:, par * 4:(par + 1) * 4], in_=conv_w[c0:c0 + 128, :]),
                inc=sCW[par], inc_k=16, waits=[r_cw[par]])
            wlb = b.op("sync", lambda s, c0=c0, par=par: s.dma_start(
                out=cb_sb[:, par:par + 1], in_=col1(conv_b[c0:c0 + 128])),
                inc=sCB[par], inc_k=16, waits=[r_cb[par]])

            hid = hid_sb[:, par * TE:(par + 1) * TE]
            gbuf = gb_sb[:, par * T:(par + 1) * T]
            hsb = hsb_sb[:, par * T:(par + 1) * T]
            cvb = cv_sb[:, par * T:(par + 1) * T]
            whp = wh_sb[:, par * 2048:(par + 1) * 2048]
            wgp = wg_sb[:, par * 2048:(par + 1) * 2048]
            wxpp = wxp_sb[:, par * 160:(par + 1) * 160]

            v_mm = None
            for k in range(16):
                v_mm = b.op("tensor", lambda t, k=k, whp=whp: t.matmul(
                    pB[:, 0:3], whp[:, k * 128:(k + 1) * 128], x_sb[:, k * TE:k * TE + 3],
                    start=(k == 0), stop=(k == 15)),
                    inc=sPE if k == 15 else None,
                    waits=([wlh, v_xload, r_pB] if k == 0 else []))
            v_bnd = b.op("scalar", lambda s, hid=hid: s.copy(out=hid[:, 0:3], in_=pB[:, 0:3]),
                         inc=sACT, waits=[v_mm, r_hid[par]])
            r_pB.set(v_bnd)
            hid_writes = [v_bnd]

            v_gev_list = []
            for j in range(NCH):
                t0 = 3 + j * NC
                v_mm = None
                for k in range(16):
                    v_mm = b.op("tensor", lambda t, k=k, t0=t0, whp=whp: t.matmul(
                        pH[:, :], whp[:, k * 128:(k + 1) * 128], x_sb[:, k * TE + t0:k * TE + t0 + NC],
                        start=(k == 0), stop=(k == 15)),
                        inc=sPE if k == 15 else None,
                        waits=([r_pH] if k == 0 else []))
                v_ev = b.op("scalar", lambda s, t0=t0, hid=hid: s.copy(out=hid[:, t0:t0 + NC], in_=pH[:, :]),
                            inc=sACT, waits=[v_mm] + ([r_hid[par]] if j == 0 else []))
                r_pH.set(v_ev)
                hid_writes.append(v_ev)
                v_gm = None
                for k in range(16):
                    v_gm = b.op("tensor", lambda t, k=k, t0=t0, wgp=wgp: t.matmul(
                        pG[:, :], wgp[:, k * 128:(k + 1) * 128], x_sb[:, k * TE + t0:k * TE + t0 + NC],
                        start=(k == 0), stop=(k == 15)),
                        inc=sPE if k == 15 else None,
                        waits=([wlg, r_pG] if k == 0 else []))
                v_gev = b.op("vector", lambda v, j=j, gbuf=gbuf: v.tensor_copy(gbuf[:, j * NC:(j + 1) * NC], pG[:, :]),
                             inc=sDVE, waits=[v_gm] + ([r_gb[par]] if j == 0 else []))
                r_pG.set(v_gev)
                v_gev_list.append(v_gev)
            last_hid_mm = v_mm
            last_gate_mm = v_gm
            v_gst = b.op("gpsimd", lambda g, c0=c0, gbuf=gbuf: g.dma_start(out=gate_o[c0:c0 + 128, :], in_=gbuf[:, :]),
                         inc=sGST[par], inc_k=16, waits=[v_gev_list[-1]])
            r_gb[par].set(v_gst)
            r_wh[par].set(last_hid_mm)
            r_wg[par].set(last_gate_mm)

            v_c = b.op("vector", lambda v, hid=hid, cvb=cvb, par=par: v.tensor_scalar(
                cvb[:, :], hid[:, 0:T], cw_sb[:, par * 4:par * 4 + 1], None, ALU.mult),
                inc=sDVE, waits=hid_writes + [wlc, r_cv[par]])
            for k in range(1, 4):
                v_c = b.op("vector", lambda v, k=k, hid=hid, cvb=cvb, par=par: v.scalar_tensor_tensor(
                    cvb[:, :], hid[:, k:k + T], cw_sb[:, par * 4 + k:par * 4 + k + 1], cvb[:, :], ALU.mult, ALU.add),
                    inc=sDVE, waits=[v_c])
            r_hid[par].set(v_c)
            r_cw[par].set(v_c)
            v_hs = b.op("scalar", lambda s, cvb=cvb, hsb=hsb, par=par: s.activation(
                out=hsb[:, :], in_=cvb[:, :], func=AF.Silu, bias=cb_sb[:, par:par + 1]),
                inc=sACT, waits=[v_c, wlb, r_hsb[par]])
            r_cv[par].set(v_hs)
            r_cb[par].set(v_hs)
            v_hst = b.op("gpsimd", lambda g, c0=c0, hsb=hsb: g.dma_start(out=hs_o[c0:c0 + 128, :], in_=hsb[:, :]),
                         inc=sHST[par], inc_k=16, waits=[v_hs])

            for j in range(NCH):
                t0 = j * NC
                vx = b.op("tensor", lambda t, t0=t0, wxpp=wxpp, hsb=hsb: t.matmul(
                    pX[:, :], wxpp[:, 0:128], hsb[:, t0:t0 + NC], start=True, stop=True),
                    inc=sPE, waits=[v_hs, wlx, r_pX])
                v_a0 = b.op("vector", lambda v, t0=t0: v.tensor_add(sp0_sb[:, t0:t0 + NC], sp0_sb[:, t0:t0 + NC], pX[:, :]),
                            inc=sDVE, waits=[vx, v_sp0])
                r_pX.set(v_a0)
                vx2 = b.op("tensor", lambda t, t0=t0, wxpp=wxpp, hsb=hsb: t.matmul(
                    pX2[:, :], wxpp[:, 128:160], hsb[:, t0:t0 + NC], start=True, stop=True),
                    inc=sPE, waits=[r_pX2])
                v_a1 = b.op("vector", lambda v, t0=t0: v.tensor_add(sp1_sb[:, t0:t0 + NC], sp1_sb[:, t0:t0 + NC], pX2[:, :]),
                            inc=sDVE, waits=[vx2, v_sp1])
                r_pX2.set(v_a1)
                v_xp_last = (v_a1, vx2)
            r_wxp[par].set(v_xp_last[1])
            r_hsb[par].set(v_xp_last[1], v_hst)
        v_spdone = v_xp_last[0]
        hs_all_stored = [sHST[0].total(), sHST[1].total()]

        stA.close()
        BAR = [sX.total(), sWH[0].total(), sWH[1].total(), sWG[0].total(), sWG[1].total(),
               sWXP[0].total(), sWXP[1].total(), sCW[0].total(), sCW[1].total(),
               sCB[0].total(), sCB[1].total(), sGST[0].total(), sGST[1].total(),
               sHST[0].total(), sHST[1].total(),
               (sPE, sPE.n), (sACT, sACT.n), (sDVE, sDVE.n), (sPOOL, sPOOL.n)]
        sq_sb = st.enter_context(nc.sbuf_tensor("sq_sb", [128, T], F32, allow_name_mangling=True))
        row_sb = st.enter_context(nc.sbuf_tensor("row_sb", [1, T], F32, allow_name_mangling=True))
        rbc_sb = st.enter_context(nc.sbuf_tensor("rbc_sb", [128, T], F32, allow_name_mangling=True))
        dtnf_sb = st.enter_context(nc.sbuf_tensor("dtnf_sb", [128, T], F32, allow_name_mangling=True))
        dtn_sb = st.enter_context(nc.sbuf_tensor("dtn_sb", [128, T], BF16, allow_name_mangling=True))
        rbc2_sb = st.enter_context(nc.sbuf_tensor("rbc2_sb", [2, T], F32, allow_name_mangling=True))
        sq32_sb = st.enter_context(nc.sbuf_tensor("sq32_sb", [32, T], F32, allow_name_mangling=True))
        bc32_sb = st.enter_context(nc.sbuf_tensor("bc32_sb", [32, T], F32, allow_name_mangling=True))
        dt_sb = st.enter_context(nc.sbuf_tensor("dt_sb", [128, 2 * T], F32, allow_name_mangling=True))
        hsf_sb = st.enter_context(nc.sbuf_tensor("hsf_sb", [128, 2 * T], F32, allow_name_mangling=True))
        dtu_sb = st.enter_context(nc.sbuf_tensor("dtu_sb", [128, 2 * T], BF16, allow_name_mangling=True))
        z1_sb = st.enter_context(nc.sbuf_tensor("z1_sb", [128, 2 * T], F32, allow_name_mangling=True))
        z2_sb = st.enter_context(nc.sbuf_tensor("z2_sb", [128, 2 * T], F32, allow_name_mangling=True))
        wdt_sb = st.enter_context(nc.sbuf_tensor("wdt_sb", [128, 2 * 128], BF16, allow_name_mangling=True))
        db_sb = st.enter_context(nc.sbuf_tensor("db_sb", [128, 2], F32, allow_name_mangling=True))

        v_sq = b.op("scalar", lambda s: s.activation(out=sq_sb[:, :], in_=sp0_sb[:, :], func=AF.Square),
                    inc=sACT, waits=[v_spdone] + BAR)
        v_rcp = None
        for j in range(NCH):
            v_rmm = b.op("tensor", lambda t, j=j: t.matmul(
                pR[0:1, :], ones128_sb[:, 0:1], sq_sb[:, j * NC:(j + 1) * NC], start=True, stop=True),
                inc=sPE, waits=[v_sq, v_ones1, v_ones128, r_pR])
            v_rcp = b.op("vector", lambda v, j=j: v.tensor_copy(row_sb[0:1, j * NC:(j + 1) * NC], pR[0:1, :]),
                         inc=sDVE, waits=[v_rmm] + BAR)
            r_pR.set(v_rcp)
        v_ln = b.op("scalar", lambda s: s.activation(out=row_sb[0:1, :], in_=row_sb[0:1, :], func=AF.Ln, scale=1.0 / DTR, bias=1e-6),
                    inc=sACT, waits=[v_rcp])
        v_ex = b.op("scalar", lambda s: s.activation(out=row_sb[0:1, :], in_=row_sb[0:1, :], func=AF.Exp, scale=-0.5),
                    inc=sACT, waits=[v_ln])
        v_bc = None
        for j in range(NCH):
            v_bcm = b.op("tensor", lambda t, j=j: t.matmul(
                pR[:, :], ones1_sb[0:1, :], row_sb[0:1, j * NC:(j + 1) * NC], start=True, stop=True),
                inc=sPE, waits=[v_ex, r_pR])
            v_bc = b.op("vector", lambda v, j=j: v.tensor_copy(rbc_sb[:, j * NC:(j + 1) * NC], pR[:, :]),
                        inc=sDVE, waits=[v_bcm] + BAR)
            r_pR.set(v_bc)
        v_dtnf = b.op("vector", lambda v: v.tensor_mul(dtnf_sb[:, :], sp0_sb[:, :], rbc_sb[:, :]),
                      inc=sDVE, waits=[v_bc] + BAR)
        v_dtn = b.op("vector", lambda v: v.tensor_copy(dtn_sb[:, :], dtnf_sb[:, :]),
                     inc=sDVE, waits=[v_dtnf] + BAR)

        v_sq2 = b.op("scalar", lambda s: s.activation(out=sq32_sb[:, :], in_=sp1_sb[:, :], func=AF.Square),
                     inc=sACT, waits=[v_spdone] + BAR)
        v_c2 = None
        for j in range(NCH):
            v_m2 = b.op("tensor", lambda t, j=j: t.matmul(
                pR[0:2, :], sel_sb[:, :], sq32_sb[:, j * NC:(j + 1) * NC], start=True, stop=True),
                inc=sPE, waits=[v_sq2, v_sel, r_pR])
            v_c2 = b.op("vector", lambda v, j=j: v.tensor_copy(rbc2_sb[:, j * NC:(j + 1) * NC], pR[0:2, :]),
                        inc=sDVE, waits=[v_m2] + BAR)
            r_pR.set(v_c2)
        v_ln2 = b.op("scalar", lambda s: s.activation(out=rbc2_sb[:, :], in_=rbc2_sb[:, :], func=AF.Ln, scale=1.0 / NST, bias=1e-6),
                     inc=sACT, waits=[v_c2])
        v_ex2 = b.op("scalar", lambda s: s.activation(out=rbc2_sb[:, :], in_=rbc2_sb[:, :], func=AF.Exp, scale=-0.5),
                     inc=sACT, waits=[v_ln2])
        v_c3 = None
        for j in range(NCH):
            v_m3 = b.op("tensor", lambda t, j=j: t.matmul(
                pX2[:, :], selbc_sb[:, :], rbc2_sb[:, j * NC:(j + 1) * NC], start=True, stop=True),
                inc=sPE, waits=[v_ex2, v_selbc, r_pX2])
            v_c3 = b.op("vector", lambda v, j=j: v.tensor_copy(bc32_sb[:, j * NC:(j + 1) * NC], pX2[:, :]),
                        inc=sDVE, waits=[v_m3] + BAR)
            r_pX2.set(v_c3)
        v_bc32 = b.op("vector", lambda v: v.tensor_mul(bc32_sb[:, :], sp1_sb[:, :], bc32_sb[:, :]),
                      inc=sDVE, waits=[v_c3])
        b.op("gpsimd", lambda g: g.dma_start(out=bn_o[:, :], in_=bc32_sb[0:16, :]),
             inc=sMISC, inc_k=16, waits=[v_bc32])
        b.op("gpsimd", lambda g: g.dma_start(out=cn_o[:, :], in_=bc32_sb[16:32, :]),
             inc=sMISC, inc_k=16, waits=[v_bc32])

        for c in range(NT):
            par = c & 1
            c0 = c * 128
            wl = b.op("gpsimd", lambda g, c0=c0, par=par: g.dma_start(
                out=wdt_sb[:, par * 128:(par + 1) * 128], in_=w_dt[:, c0:c0 + 128]),
                inc=sWDT[par], inc_k=16, waits=[r_wdt[par]] + BAR)
            bl = b.op("sync", lambda s, c0=c0, par=par: s.dma_start(
                out=db_sb[:, par:par + 1], in_=col1(b_dt[c0:c0 + 128])),
                inc=sDB[par], inc_k=16, waits=[r_db[par]] + BAR)
            hl = b.op("gpsimd", lambda g, c0=c0, par=par: g.dma_start(
                out=hsf_sb[:, par * T:(par + 1) * T], in_=hs_o[c0:c0 + 128, :]),
                inc=sHSF[par], inc_k=16, waits=hs_all_stored + [r_hsf[par]] + BAR)
            dtb = dt_sb[:, par * T:(par + 1) * T]
            hsf = hsf_sb[:, par * T:(par + 1) * T]
            dtub = dtu_sb[:, par * T:(par + 1) * T]
            z1 = z1_sb[:, par * T:(par + 1) * T]
            z2 = z2_sb[:, par * T:(par + 1) * T]
            v_adds = []
            v_mm_last = None
            for j in range(NCH):
                t0 = j * NC
                vm = b.op("tensor", lambda t, par=par, t0=t0: t.matmul(
                    pH[:, :], wdt_sb[:, par * 128:(par + 1) * 128], dtn_sb[:, t0:t0 + NC], start=True, stop=True),
                    inc=sPE, waits=[wl, v_dtn, r_pH])
                v_mm_last = vm
                va = b.op("scalar", lambda s, t0=t0, z1=z1, par=par: s.activation(
                    out=z1[:, t0:t0 + NC], in_=pH[:, :], func=AF.Abs, bias=db_sb[:, par:par + 1]),
                    inc=sACT, waits=[vm, bl] + (([r_z1[par]] + BAR) if j == 0 else []))
                vr = b.op("vector", lambda v, t0=t0, dtb=dtb, par=par: v.tensor_scalar(
                    dtb[:, t0:t0 + NC], pH[:, :], db_sb[:, par:par + 1], 0.0, ALU.add, ALU.max),
                    inc=sDVE, waits=[vm, bl, va] + ((BAR + [r_dt[par]]) if j == 0 else []))
                r_pH.set(va, vr)
                if DT_STAGE >= 3:
                    ve = b.op("scalar", lambda s, t0=t0, z1=z1, z2=z2: s.activation(
                        out=z2[:, t0:t0 + NC], in_=z1[:, t0:t0 + NC], func=AF.Exp, scale=-1.0),
                        inc=sACT, waits=[va] + ((BAR + [r_z2[par]]) if j == 0 else []))
                    vl = b.op("scalar", lambda s, t0=t0, z2=z2: s.activation(
                        out=z2[:, t0:t0 + NC], in_=z2[:, t0:t0 + NC], func=AF.Ln, bias=1.0),
                        inc=sACT, waits=[ve])
                else:
                    ve = vl = va
                if DT_STAGE >= 4:
                    vadd = b.op("vector", lambda v, t0=t0, dtb=dtb, z2=z2: v.tensor_add(
                        dtb[:, t0:t0 + NC], dtb[:, t0:t0 + NC], z2[:, t0:t0 + NC]),
                        inc=sDVE, waits=[vl, vr])
                else:
                    vadd = vr
                v_adds.append(vadd)
            r_wdt[par].set(v_mm_last)
            r_db[par].set(va, vr)
            r_z1[par].set(ve)
            r_z2[par].set(v_adds[-1])
            v_dst = b.op("gpsimd", lambda g, c0=c0, dtb=dtb: g.dma_start(out=dt_o[c0:c0 + 128, :], in_=dtb[:, :]),
                         inc=sDST[par], inc_k=16, waits=v_adds[-1:])
            if DT_STAGE >= 6:
                v_dtu = b.op("vector", lambda v, dtb=dtb, hsf=hsf, dtub=dtub: v.tensor_mul(dtub[:, :], dtb[:, :], hsf[:, :]),
                             inc=sDVE, waits=[hl] + v_adds + [r_dtu[par]] + BAR)
                r_hsf[par].set(v_dtu)
                r_dt[par].set(v_dtu, v_dst)
                v_ust = b.op("gpsimd", lambda g, c0=c0, dtub=dtub: g.dma_start(out=dtu_o[c0:c0 + 128, :], in_=dtub[:, :]),
                             inc=sUST[par], inc_k=16, waits=[v_dtu])
                r_dtu[par].set(v_ust)
            else:
                r_dt[par].set(v_dst)

        finals = [sMISC.total(), sGST[0].total(), sGST[1].total(), sHST[0].total(), sHST[1].total(),
                  sDST[0].total(), sDST[1].total(), sUST[0].total(), sUST[1].total()]
        FINALS = finals
    return FINALS



def emit_p2(nc, b, D):
    CH = 512
    TT = NCORES * TPC
    NG = CH // 8
    CK = 2048
    NCK = TT // CK
    NC = 512
    NSUB = CK // NC

    dtg = D["dtg"]
    dtug = D["dtug"]
    bng = D["bng"]
    cng = D["cng"]
    ac_d = D["acols"]
    cid_d = D["cid"]
    y_d = D["y_i"]

    st = ExitStack()
    with st:
        rep01f = st.enter_context(nc.sbuf_tensor("p2_rep01f", [8, 128], F32, allow_name_mangling=True))
        rep01b = st.enter_context(nc.sbuf_tensor("p2_rep01b", [8, 128], BF16, allow_name_mangling=True))
        rep01tb = st.enter_context(nc.sbuf_tensor("p2_rep01tb", [128, 8], BF16, allow_name_mangling=True))
        sel16 = st.enter_context(nc.sbuf_tensor("p2_sel16", [16, 128], F32, allow_name_mangling=True))
        io8r = st.enter_context(nc.sbuf_tensor("p2_io8r", [8, 128], F32, allow_name_mangling=True))
        io8c = st.enter_context(nc.sbuf_tensor("p2_io8c", [8, 1], F32, allow_name_mangling=True))
        io128r = st.enter_context(nc.sbuf_tensor("p2_io128r", [128, 8], F32, allow_name_mangling=True))
        io128c = st.enter_context(nc.sbuf_tensor("p2_io128c", [128, 1], F32, allow_name_mangling=True))
        io16r = st.enter_context(nc.sbuf_tensor("p2_io16r", [16, 128], F32, allow_name_mangling=True))
        io16c = st.enter_context(nc.sbuf_tensor("p2_io16c", [16, 1], F32, allow_name_mangling=True))
        ztmp = st.enter_context(nc.sbuf_tensor("p2_ztmp", [128, 16], F32, allow_name_mangling=True))
        acs = st.enter_context(nc.sbuf_tensor("p2_acs", [128, 64], F32, allow_name_mangling=True))
        cidb = st.enter_context(nc.sbuf_tensor("p2_cidb", [8, 1], F32, allow_name_mangling=True))
        base8 = st.enter_context(nc.sbuf_tensor("p2_base8", [8, 1], F32, allow_name_mangling=True))
        idx8 = st.enter_context(nc.sbuf_tensor("p2_idx8", [8, 2], I32, allow_name_mangling=True))
        dt8 = st.enter_context(nc.sbuf_tensor("p2_dt8", [8, 2 * CK], F32, allow_name_mangling=True))
        dtu8 = st.enter_context(nc.sbuf_tensor("p2_dtu8", [8, 2 * CK], BF16, allow_name_mangling=True))
        bn_sb = st.enter_context(nc.sbuf_tensor("p2_bn", [16, 2 * CK], F32, allow_name_mangling=True))
        cn_sb = st.enter_context(nc.sbuf_tensor("p2_cn", [16, 2 * CK], F32, allow_name_mangling=True))
        brep = st.enter_context(nc.sbuf_tensor("p2_brep", [128, CK], F32, allow_name_mangling=True))
        crep = st.enter_context(nc.sbuf_tensor("p2_crep", [128, CK], F32, allow_name_mangling=True))
        da_sb = st.enter_context(nc.sbuf_tensor("p2_da", [128, CK], F32, allow_name_mangling=True))
        dbu_sb = st.enter_context(nc.sbuf_tensor("p2_dbu", [128, CK], F32, allow_name_mangling=True))
        s_sb = st.enter_context(nc.sbuf_tensor("p2_s", [128, CK], F32, allow_name_mangling=True))
        prod_sb = st.enter_context(nc.sbuf_tensor("p2_prod", [128, CK], BF16, allow_name_mangling=True))
        y_sb = st.enter_context(nc.sbuf_tensor("p2_y", [8, 2 * CK], F32, allow_name_mangling=True))
        carry = st.enter_context(nc.sbuf_tensor("p2_carry", [128, NG], F32, allow_name_mangling=True))

        pA = st.enter_context(nc.psum_tensor("p2_pA", [128, CK], F32, allow_name_mangling=True))
        pU = st.enter_context(nc.psum_tensor("p2_pU", [128, CK], F32, allow_name_mangling=True))

        sPE, sACT, sDVE, sPOOL = b.sem("sPE"), b.sem("sACT"), b.sem("sDVE"), b.sem("sPOOL")
        sAC = b.sem("sAC"); sCID = b.sem("sCID")
        sDT8 = b.sems(2, "sDT8"); sDTU8 = b.sems(2, "sDTU8")
        sBN = b.sems(2, "sBN"); sCN = b.sems(2, "sCN")
        sYST = b.sems(2, "sYST")

        r_pA = Res(); r_pU = Res()
        r_dt8 = [Res(), Res()]; r_dtu8 = [Res(), Res()]
        r_bn = [Res(), Res()]; r_cn = [Res(), Res()]
        r_brep = Res(); r_crep = Res()
        r_da = Res(); r_dbu = Res(); r_s = Res(); r_prod = Res()
        r_y = [Res(), Res()]
        r_idx = [Res(), Res()]
        last_dl = [None, None]; last_ul = [None, None]

        v_i8r = b.op("gpsimd", lambda g: g.iota(io8r[:, :], pattern=[[1, 8], [0, 16]], channel_multiplier=0, allow_small_or_imprecise_dtypes=True), inc=sPOOL)
        v_i8c = b.op("gpsimd", lambda g: g.iota(io8c[:, :], pattern=[[0, 1]], channel_multiplier=1, allow_small_or_imprecise_dtypes=True), inc=sPOOL)
        v_i128r = b.op("gpsimd", lambda g: g.iota(io128r[:, :], pattern=[[1, 8]], channel_multiplier=0, allow_small_or_imprecise_dtypes=True), inc=sPOOL)
        v_i128c = b.op("gpsimd", lambda g: g.iota(io128c[:, :], pattern=[[0, 1]], channel_multiplier=1, allow_small_or_imprecise_dtypes=True), inc=sPOOL)
        v_i16r = b.op("gpsimd", lambda g: g.iota(io16r[:, :], pattern=[[0, 8], [1, 16]], channel_multiplier=0, allow_small_or_imprecise_dtypes=True), inc=sPOOL)
        v_i16c = b.op("gpsimd", lambda g: g.iota(io16c[:, :], pattern=[[0, 1]], channel_multiplier=1, allow_small_or_imprecise_dtypes=True), inc=sPOOL)
        v_r01 = b.op("vector", lambda v: v.tensor_scalar(rep01f[:, :], io8r[:, :], io8c[:, 0:1], None, ALU.is_equal),
                     inc=sDVE, waits=[v_i8r, v_i8c])
        v_r01b = b.op("vector", lambda v: v.tensor_copy(rep01b[:, :], rep01f[:, :]), inc=sDVE, waits=[v_r01])
        v_t1 = b.op("vector", lambda v: v.tensor_scalar(ztmp[:, 0:8], io128r[:, :], 16.0, io128c[:, 0:1], ALU.mult, ALU.subtract),
                    inc=sDVE, waits=[v_i128r, v_i128c])
        v_t2 = b.op("vector", lambda v: v.tensor_scalar(ztmp[:, 8:16], ztmp[:, 0:8], 0.0, None, ALU.is_le),
                    inc=sDVE, waits=[v_t1])
        v_t3 = b.op("vector", lambda v: v.tensor_scalar(ztmp[:, 0:8], ztmp[:, 0:8], -16.0, None, ALU.is_gt),
                    inc=sDVE, waits=[v_t2])
        v_r01t = b.op("vector", lambda v: v.tensor_mul(rep01tb[:, :], ztmp[:, 0:8], ztmp[:, 8:16]),
                      inc=sDVE, waits=[v_t3])
        v_s16 = b.op("vector", lambda v: v.tensor_scalar(sel16[:, :], io16r[:, :], io16c[:, 0:1], None, ALU.is_equal),
                     inc=sDVE, waits=[v_i16r, v_i16c])

        v_acl = b.op("gpsimd", lambda g: g.dma_start(out=acs[:, :], in_=ac_d[:, :]), inc=sAC, inc_k=16)
        v_ace = b.op("scalar", lambda s: s.activation(out=acs[:, :], in_=acs[:, :], func=AF.Exp),
                     inc=sACT, waits=[v_acl])
        v_acn = b.op("vector", lambda v: v.tensor_scalar(acs[:, :], acs[:, :], -1.0, None, ALU.mult),
                     inc=sDVE, waits=[v_ace])
        v_cl = b.op("gpsimd", lambda g: g.dma_start(out=cidb[:, :], in_=D["cid"].unsqueeze(0).partition_broadcast(8)),
                    inc=sCID, inc_k=16)
        v_b8 = b.op("vector", lambda v: v.tensor_scalar(base8[:, :], cidb[:, :], 512.0, io8c[:, 0:1], ALU.mult, ALU.add),
                    inc=sDVE, waits=[v_cl, v_i8c])

        for ck in range(NCK):
            ckpar = ck & 1
            bl = b.op("sync", lambda s, ck=ck, ckpar=ckpar: s.dma_start(
                out=bn_sb[:, ckpar * CK:(ckpar + 1) * CK], in_=bng[ck * 16:(ck + 1) * 16, :]),
                inc=sBN[ckpar], inc_k=16, waits=[r_bn[ckpar]])
            cl = b.op("sync", lambda s, ck=ck, ckpar=ckpar: s.dma_start(
                out=cn_sb[:, ckpar * CK:(ckpar + 1) * CK], in_=cng[ck * 16:(ck + 1) * 16, :]),
                inc=sCN[ckpar], inc_k=16, waits=[r_cn[ckpar]])
            v_br = None
            for sub in range(NSUB):
                vmB = b.op("tensor", lambda t, sub=sub, ckpar=ckpar: t.matmul(
                    pA[:, sub * NC:(sub + 1) * NC], sel16[:, :], bn_sb[:, ckpar * CK + sub * NC:ckpar * CK + (sub + 1) * NC],
                    start=True, stop=True),
                    inc=sPE, waits=[bl, v_s16, r_pA])
            v_brc = b.op("vector", lambda v: v.tensor_copy(brep[:, :], pA[:, :]), inc=sDVE, waits=[vmB, r_brep])
            r_pA.set(v_brc)
            for sub in range(NSUB):
                vmC = b.op("tensor", lambda t, sub=sub, ckpar=ckpar: t.matmul(
                    pA[:, sub * NC:(sub + 1) * NC], sel16[:, :], cn_sb[:, ckpar * CK + sub * NC:ckpar * CK + (sub + 1) * NC],
                    start=True, stop=True),
                    inc=sPE, waits=[cl, v_s16, r_pA])
            v_crc = b.op("vector", lambda v: v.tensor_copy(crep[:, :], pA[:, :]), inc=sDVE, waits=[vmC, r_crep])
            r_pA.set(v_crc)
            r_bn[ckpar].set(vmB)
            r_cn[ckpar].set(vmC)
            b_users = []
            c_users = []

            for g in range(NG):
                gpar = g & 1
                v_ix = b.op("vector", lambda v, ck=ck, g=g, gpar=gpar: v.tensor_scalar(
                    idx8[:, gpar:gpar + 1], base8[:, :], float(ck * 4096 + 8 * g), None, ALU.add),
                    inc=sDVE, waits=[v_b8, r_idx[gpar]])
                dl = b.op("gpsimd", lambda gg, gpar=gpar: gg.indirect_dma_start(
                    out=dt8[:, gpar * CK:(gpar + 1) * CK], out_offset=None,
                    in_=dtg[:, :], in_offset=bass.IndirectOffsetOnAxis(idx8[:, gpar:gpar + 1], 0)),
                    inc=sDT8[gpar], inc_k=16, waits=[v_ix, r_dt8[gpar], last_dl[gpar]])
                ul = b.op("gpsimd", lambda gg, gpar=gpar: gg.indirect_dma_start(
                    out=dtu8[:, gpar * CK:(gpar + 1) * CK], out_offset=None,
                    in_=dtug[:, :], in_offset=bass.IndirectOffsetOnAxis(idx8[:, gpar:gpar + 1], 0)),
                    inc=sDTU8[gpar], inc_k=16, waits=[v_ix, r_dtu8[gpar], last_ul[gpar]])
                last_dl[gpar] = dl
                last_ul[gpar] = ul
                r_idx[gpar].set(dl, ul)
                v_am = None
                for sub in range(NSUB):
                    v_am = b.op("tensor", lambda t, g=g, sub=sub, gpar=gpar: t.matmul(
                        pA[:, sub * NC:(sub + 1) * NC], rep01f[:, :],
                        dt8[:, gpar * CK + sub * NC:gpar * CK + (sub + 1) * NC], start=True, stop=True),
                        inc=sPE, waits=([dl, v_r01, r_pA] if sub == 0 else []))
                v_da = b.op("scalar", lambda s, g=g: s.activation(out=da_sb[:, :], in_=pA[:, :], func=AF.Exp, scale=acs[:, g:g + 1]),
                            inc=sACT, waits=[v_am, r_da, v_acn])
                r_pA.set(v_da)
                v_um = None
                for sub in range(NSUB):
                    v_um = b.op("tensor", lambda t, g=g, sub=sub, gpar=gpar: t.matmul(
                        pU[:, sub * NC:(sub + 1) * NC], rep01b[:, :],
                        dtu8[:, gpar * CK + sub * NC:gpar * CK + (sub + 1) * NC], start=True, stop=True),
                        inc=sPE, waits=([ul, v_r01b, r_pU] if sub == 0 else []))
                v_db = b.op("vector", lambda v: v.tensor_mul(dbu_sb[:, :], pU[:, :], brep[:, :]),
                            inc=sDVE, waits=[v_um, v_brc, r_dbu])
                r_pU.set(v_db)
                b_users.append(v_db)
                if ck % 4 == 0:
                    v_sc = b.op("vector", lambda v: v.tensor_tensor_scan(
                        s_sb[:, :], da_sb[:, :], dbu_sb[:, :], 0.0, ALU.mult, ALU.add),
                        inc=sDVE, waits=[v_da, v_db, r_s])
                else:
                    v_sc = b.op("vector", lambda v, g=g: v.tensor_tensor_scan(
                        s_sb[:, :], da_sb[:, :], dbu_sb[:, :], carry[:, g:g + 1], ALU.mult, ALU.add),
                        inc=sDVE, waits=[v_da, v_db, r_s])
                r_da.set(v_sc)
                r_dbu.set(v_sc)
                v_cw = b.op("vector", lambda v, g=g: v.tensor_copy(carry[:, g:g + 1], s_sb[:, CK - 1:CK]),
                            inc=sDVE, waits=[v_sc])
                v_pr = b.op("vector", lambda v: v.tensor_mul(prod_sb[:, :], s_sb[:, :], crep[:, :]),
                            inc=sDVE, waits=[v_sc, v_crc, r_prod])
                r_s.set(v_pr, v_cw)
                c_users.append(v_pr)
                v_ym = None
                for sub in range(NSUB):
                    v_ym = b.op("tensor", lambda t, sub=sub: t.matmul(
                        pA[0:8, sub * NC:(sub + 1) * NC], rep01tb[:, :],
                        prod_sb[:, sub * NC:(sub + 1) * NC], start=True, stop=True),
                        inc=sPE, waits=([v_pr, v_r01t, r_pA] if sub == 0 else []))
                v_ye = b.op("vector", lambda v, gpar=gpar: v.tensor_copy(y_sb[:, gpar * CK:(gpar + 1) * CK], pA[0:8, :]),
                            inc=sDVE, waits=[v_ym, r_y[gpar]])
                r_pA.set(v_ye)
                r_prod.set(v_ym)
                v_yst = b.op("gpsimd", lambda gg, g=g, ck=ck, gpar=gpar: gg.dma_start(
                    out=y_d.rearrange("(c k) t -> c k t", k=8)[g * 8:(g + 1) * 8, ck, :],
                    in_=y_sb[:, gpar * CK:(gpar + 1) * CK]),
                    inc=sYST[gpar], inc_k=16, waits=[v_ye])
                r_y[gpar].set(v_yst)
                r_dt8[gpar].set(v_am)
                r_dtu8[gpar].set(v_um)
            r_brep.set(*b_users)
            r_crep.set(*c_users)

        FINALS = [sYST[0].total(), sYST[1].total()]
    return FINALS


def emit_p3(nc, b, D):
    T = TPC
    NC = 512
    NCH = T // NC
    NT = DIN // 128
    NM = DM // 128
    hs_d = D["hs"]
    gt_d = D["gate"]
    x_d = D["xT"]
    wo_d = D["w_out"]
    dp_d = D["d_par"]
    x1_d = D["x1"]
    xa_i = D["xa_i"]
    yg = D["yg"]
    cid_d = D["cid"]

    st = ExitStack()
    st1 = ExitStack()
    st2 = ExitStack()
    with st:
        io128c_sb = st.enter_context(nc.sbuf_tensor("p3_io128c", [128, 1], F32, allow_name_mangling=True))
        cidb_sb = st.enter_context(nc.sbuf_tensor("p3_cidb", [128, 1], F32, allow_name_mangling=True))
        yb_sb = st.enter_context(nc.sbuf_tensor("p3_ybase", [128, 1], F32, allow_name_mangling=True))
        yix_sb = st.enter_context(nc.sbuf_tensor("p3_yix", [128, 2], I32, allow_name_mangling=True))
        csts_sb = st.enter_context(nc.sbuf_tensor("csts_sb", [128, 8], F32, allow_name_mangling=True))
        ones1_sb = st.enter_context(nc.sbuf_tensor("ones1_sb", [1, 128], F32, allow_name_mangling=True))
        ones128_sb = st.enter_context(nc.sbuf_tensor("ones128_sb", [128, 1], F32, allow_name_mangling=True))
        sum_sb = st.enter_context(nc.sbuf_tensor("sum_sb", [1, T], F32, allow_name_mangling=True))
        ssq_sb = st.enter_context(nc.sbuf_tensor("ssq_sb", [1, T], F32, allow_name_mangling=True))
        tmp_sb = st.enter_context(nc.sbuf_tensor("tmp_sb", [1, T], F32, allow_name_mangling=True))
        yo_sb = st.enter_context(nc.sbuf_tensor("yo_sb", [128, NT * T], BF16, allow_name_mangling=True))
        yt_sb = st1.enter_context(nc.sbuf_tensor("yt_sb", [128, 2 * T], F32, allow_name_mangling=True))
        hsf_sb = st1.enter_context(nc.sbuf_tensor("hsf_sb", [128, 2 * T], F32, allow_name_mangling=True))
        gt_sb = st1.enter_context(nc.sbuf_tensor("gt_sb", [128, 2 * T], BF16, allow_name_mangling=True))
        sg_sb = st1.enter_context(nc.sbuf_tensor("sg_sb", [128, T], F32, allow_name_mangling=True))
        dp_sb = st1.enter_context(nc.sbuf_tensor("dp_sb", [128, 2], F32, allow_name_mangling=True))

        pM = [st.enter_context(nc.psum_tensor(f"pM{i}", [128, T], F32, allow_name_mangling=True)) for i in range(2)]

        sPE, sACT, sDVE, sPOOL = b.sem("sPE"), b.sem("sACT"), b.sem("sDVE"), b.sem("sPOOL")
        sY = b.sems(2, "sY"); sHS = b.sems(2, "sHS"); sGT = b.sems(2, "sGT"); sDP = b.sems(2, "sDP")
        sWO = b.sems(2, "sWO"); sXT = b.sems(2, "sXT"); sXA = b.sems(2, "sXA")
        sXAL = b.sems(2, "sXAL"); sX1 = b.sems(2, "sX1")

        v_ones1 = b.op("gpsimd", lambda g: g.memset(ones1_sb[:, :], 1.0), inc=sPOOL)
        v_ones128 = b.op("gpsimd", lambda g: g.memset(ones128_sb[:, :], 1.0), inc=sPOOL)
        v_sum0 = b.op("gpsimd", lambda g: g.memset(sum_sb[:, :], 0.0), inc=sPOOL)
        v_ssq0 = b.op("gpsimd", lambda g: g.memset(ssq_sb[:, :], 0.0), inc=sPOOL)
        v_ioc = b.op("gpsimd", lambda g: g.iota(io128c_sb[:, :], pattern=[[0, 1]], channel_multiplier=1, allow_small_or_imprecise_dtypes=True), inc=sPOOL)
        v_cl = b.op("gpsimd", lambda g: g.dma_start(out=cidb_sb[:, :], in_=cid_d.unsqueeze(0).partition_broadcast(128)), inc=b.sem("sCID3"), inc_k=16)
        v_yb = b.op("vector", lambda v: v.tensor_scalar(yb_sb[:, :], io128c_sb[:, :], 8.0, cidb_sb[:, 0:1], ALU.mult, ALU.add),
                    inc=sDVE, waits=[v_ioc, v_cl])

        r_y = [Res(), Res()]; r_yi = [Res(), Res()]; r_hs = [Res(), Res()]; r_gt = [Res(), Res()]; r_dp = [Res(), Res()]
        r_sg = Res(); r_yt = [Res(), Res()]
        r_wo = [Res(), Res()]; r_xt = [Res(), Res()]; r_xa = [Res(), Res()]; r_sq = [Res(), Res()]
        r_xab = [Res(), Res()]; r_x1 = [Res(), Res()]
        r_pM = [Res(), Res()]

        v_yo = None
        for c in range(NT):
            par = c & 1
            c0 = c * 128
            v_yi = b.op("vector", lambda v, c=c, par=par: v.tensor_scalar(
                yix_sb[:, par:par + 1], yb_sb[:, :], float((c // 4) * 4096 + (c % 4) * 128 * 8), None, ALU.add),
                inc=sDVE, waits=[v_yb, r_yi[par]])
            yl = b.op("gpsimd", lambda g, par=par: g.indirect_dma_start(
                out=yt_sb[:, par * T:(par + 1) * T], out_offset=None,
                in_=yg[:, :], in_offset=bass.IndirectOffsetOnAxis(yix_sb[:, par:par + 1], 0)),
                inc=sY[par], inc_k=16, waits=[v_yi, r_y[par]])
            hl = b.op("gpsimd", lambda g, c0=c0, par=par: g.dma_start(
                out=hsf_sb[:, par * T:(par + 1) * T], in_=hs_d[c0:c0 + 128, :]),
                inc=sHS[par], inc_k=16, waits=[r_hs[par]])
            gl = b.op("sync", lambda s, c0=c0, par=par: s.dma_start(
                out=gt_sb[:, par * T:(par + 1) * T], in_=gt_d[c0:c0 + 128, :]),
                inc=sGT[par], inc_k=16, waits=[r_gt[par]])
            dl = b.op("sync", lambda s, c0=c0, par=par: s.dma_start(
                out=dp_sb[:, par:par + 1], in_=col1(dp_d[c0:c0 + 128])),
                inc=sDP[par], inc_k=16, waits=[r_dp[par]])
            yt = yt_sb[:, par * T:(par + 1) * T]
            hsf = hsf_sb[:, par * T:(par + 1) * T]
            gt = gt_sb[:, par * T:(par + 1) * T]
            v_yf = b.op("vector", lambda v, yt=yt, hsf=hsf, par=par: v.scalar_tensor_tensor(
                yt[:, :], hsf[:, :], dp_sb[:, par:par + 1], yt[:, :], ALU.mult, ALU.add),
                inc=sDVE, waits=[yl, hl, dl, r_yt[par]])
            r_hs[par].set(v_yf)
            r_dp[par].set(v_yf)
            r_yi[par].set(yl)
            v_sg = b.op("scalar", lambda s, gt=gt: s.activation(out=sg_sb[:, :], in_=gt[:, :], func=AF.Silu),
                        inc=sACT, waits=[gl, r_sg])
            r_gt[par].set(v_sg)
            v_yo = b.op("vector", lambda v, c=c, yt=yt: v.tensor_mul(yo_sb[:, c * T:(c + 1) * T], yt[:, :], sg_sb[:, :]),
                        inc=sDVE, waits=[v_yf, v_sg])
            r_yt[par].set(v_yo)
            r_sg.set(v_yo)
            r_y[par].set(v_yf)
        st1.close()
        BAR1 = [sY[0].total(), sY[1].total(), sHS[0].total(), sHS[1].total(),
                sGT[0].total(), sGT[1].total(), sDP[0].total(), sDP[1].total(),
                (sDVE, sDVE.n), (sACT, sACT.n)]
        xt_sb = st2.enter_context(nc.sbuf_tensor("xt_sb", [128, 2 * T], F32, allow_name_mangling=True))
        xa_sb = st2.enter_context(nc.sbuf_tensor("xa_sb", [128, 2 * T], F32, allow_name_mangling=True))
        sq_sb = st2.enter_context(nc.sbuf_tensor("sq_sb", [128, 2 * T], F32, allow_name_mangling=True))
        wo_sb = st2.enter_context(nc.sbuf_tensor("wo_sb", [128, 2 * 128], BF16, allow_name_mangling=True))

        for m in range(NM):
            par = m & 1
            m0 = m * 128
            xl = b.op("sync", lambda s, m0=m0, par=par: s.dma_start(
                out=xt_sb[:, par * T:(par + 1) * T], in_=x_d[m0:m0 + 128, :]),
                inc=sXT[par], inc_k=16, waits=[r_xt[par]] + BAR1)
            v_mm = None
            for c in range(NT):
                wpar = c & 1
                wl = b.op("gpsimd", lambda g, c=c, m0=m0, wpar=wpar: g.dma_start(
                    out=wo_sb[:, wpar * 128:(wpar + 1) * 128], in_=wo_d[c * 128:(c + 1) * 128, m0:m0 + 128]),
                    inc=sWO[wpar], inc_k=16, waits=[r_wo[wpar]] + (BAR1 if m == 0 and c < 2 else []))
                for j in range(NCH):
                    v_mm = b.op("tensor", lambda t, c=c, j=j, par=par, wpar=wpar: t.matmul(
                        pM[par][:, j * NC:(j + 1) * NC], wo_sb[:, wpar * 128:(wpar + 1) * 128],
                        yo_sb[:, c * T + j * NC:c * T + (j + 1) * NC],
                        start=(c == 0), stop=(c == NT - 1)),
                        inc=sPE if j == NCH - 1 else None,
                        waits=([wl] + ([v_yo, r_pM[par]] if c == 0 else [])))
                r_wo[wpar].set(v_mm)
            xa = xa_sb[:, par * T:(par + 1) * T]
            sq = sq_sb[:, par * T:(par + 1) * T]
            v_xa = b.op("vector", lambda v, xa=xa, par=par: v.tensor_add(xa[:, :], pM[par][:, :], xt_sb[:, par * T:(par + 1) * T]),
                        inc=sDVE, waits=[v_mm, xl, r_xa[par]])
            r_pM[par].set(v_xa)
            r_xt[par].set(v_xa)
            v_xas = b.op("gpsimd", lambda g, m0=m0, xa=xa: g.dma_start(out=xa_i[m0:m0 + 128, :], in_=xa[:, :]),
                         inc=sXA[par], inc_k=16, waits=[v_xa])
            v_sq = b.op("scalar", lambda s, xa=xa, sq=sq: s.activation(out=sq[:, :], in_=xa[:, :], func=AF.Square),
                        inc=sACT, waits=[v_xa, r_sq[par]])
            r_xa[par].set(v_sq, v_xas)
            for j in range(NCH):
                v_s1 = b.op("tensor", lambda t, j=j, xa=xa, par=par: t.matmul(
                    pM[par][0:1, j * NC:(j + 1) * NC],
                    ones128_sb[:, 0:1], xa[:, j * NC:(j + 1) * NC], start=True, stop=True),
                    inc=sPE, waits=[v_xa, v_ones128, r_pM[par]])
                v_a1 = b.op("vector", lambda v, j=j, par=par: v.tensor_add(sum_sb[0:1, j * NC:(j + 1) * NC], sum_sb[0:1, j * NC:(j + 1) * NC], pM[par][0:1, j * NC:(j + 1) * NC]),
                            inc=sDVE, waits=[v_s1, v_sum0])
                v_s2 = b.op("tensor", lambda t, j=j, sq=sq, par=par: t.matmul(
                    pM[par][0:1, j * NC:(j + 1) * NC], ones128_sb[:, 0:1], sq[:, j * NC:(j + 1) * NC], start=True, stop=True),
                    inc=sPE, waits=[v_sq, v_a1])
                v_a2 = b.op("vector", lambda v, j=j, par=par: v.tensor_add(ssq_sb[0:1, j * NC:(j + 1) * NC], ssq_sb[0:1, j * NC:(j + 1) * NC], pM[par][0:1, j * NC:(j + 1) * NC]),
                            inc=sDVE, waits=[v_s2, v_ssq0])
            r_sq[par].set(v_s2)
            r_pM[par].set(v_a2)
        v_stats = v_a2
        st2.close()
        BAR2 = [sXT[0].total(), sXT[1].total(), sWO[0].total(), sWO[1].total(),
                sXA[0].total(), sXA[1].total(), (sPE, sPE.n), (sACT, sACT.n), (sDVE, sDVE.n)]
        xab_sb = st.enter_context(nc.sbuf_tensor("xab_sb", [128, 2 * T], F32, allow_name_mangling=True))
        x1_sb = st.enter_context(nc.sbuf_tensor("x1_sb", [128, 2 * T], BF16, allow_name_mangling=True))
        mubc_sb = st.enter_context(nc.sbuf_tensor("mubc_sb", [128, T], F32, allow_name_mangling=True))
        rsbc_sb = st.enter_context(nc.sbuf_tensor("rsbc_sb", [128, T], F32, allow_name_mangling=True))

        v_mu = b.op("vector", lambda v: v.tensor_scalar(sum_sb[0:1, :], sum_sb[0:1, :], 1.0 / DM, None, ALU.mult),
                    inc=sDVE, waits=[v_stats])
        v_m2 = b.op("vector", lambda v: v.tensor_mul(tmp_sb[0:1, :], sum_sb[0:1, :], sum_sb[0:1, :]),
                    inc=sDVE, waits=[v_mu])
        v_var = b.op("vector", lambda v: v.scalar_tensor_tensor(ssq_sb[0:1, :], ssq_sb[0:1, :], 1.0 / DM, tmp_sb[0:1, :], ALU.mult, ALU.subtract),
                     inc=sDVE, waits=[v_m2])
        v_ln = b.op("scalar", lambda s: s.activation(out=ssq_sb[0:1, :], in_=ssq_sb[0:1, :], func=AF.Ln, bias=1e-5),
                    inc=sACT, waits=[v_var])
        v_rs = b.op("scalar", lambda s: s.activation(out=ssq_sb[0:1, :], in_=ssq_sb[0:1, :], func=AF.Exp, scale=-0.5),
                    inc=sACT, waits=[v_ln])
        v_mbc = None
        for j in range(NCH):
            v_bm = b.op("tensor", lambda t, j=j: t.matmul(
                pM[0][:, j * NC:(j + 1) * NC], ones1_sb[0:1, :], sum_sb[0:1, j * NC:(j + 1) * NC], start=True, stop=True),
                inc=sPE, waits=[v_mu, v_ones1, r_pM[0]])
            v_mbc = b.op("vector", lambda v, j=j: v.tensor_copy(mubc_sb[:, j * NC:(j + 1) * NC], pM[0][:, j * NC:(j + 1) * NC]),
                         inc=sDVE, waits=[v_bm] + BAR2)
            r_pM[0].set(v_mbc)
        v_rbc = None
        for j in range(NCH):
            v_rm = b.op("tensor", lambda t, j=j: t.matmul(
                pM[1][:, j * NC:(j + 1) * NC], ones1_sb[0:1, :], ssq_sb[0:1, j * NC:(j + 1) * NC], start=True, stop=True),
                inc=sPE, waits=[v_rs, v_ones1, r_pM[1]])
            v_rbc = b.op("vector", lambda v, j=j: v.tensor_copy(rsbc_sb[:, j * NC:(j + 1) * NC], pM[1][:, j * NC:(j + 1) * NC]),
                         inc=sDVE, waits=[v_rm] + BAR2)
            r_pM[1].set(v_rbc)
        for m in range(NM):
            par = m & 1
            m0 = m * 128
            al = b.op("gpsimd", lambda g, m0=m0, par=par: g.dma_start(
                out=xab_sb[:, par * T:(par + 1) * T], in_=xa_i[m0:m0 + 128, :]),
                inc=sXAL[par], inc_k=16, waits=[r_xab[par], sXA[0].total(), sXA[1].total()] + (BAR2 if m < 2 else []))
            xab = xab_sb[:, par * T:(par + 1) * T]
            x1t = x1_sb[:, par * T:(par + 1) * T]
            v_n1 = b.op("vector", lambda v, xab=xab: v.tensor_sub(xab[:, :], xab[:, :], mubc_sb[:, :]),
                        inc=sDVE, waits=[al, v_mbc])
            v_n2 = b.op("vector", lambda v, xab=xab, x1t=x1t: v.tensor_mul(x1t[:, :], xab[:, :], rsbc_sb[:, :]),
                        inc=sDVE, waits=[v_n1, v_rbc, r_x1[par]])
            r_xab[par].set(v_n2)
            v_st = b.op("gpsimd", lambda g, m0=m0, x1t=x1t: g.dma_start(out=x1_d[m0:m0 + 128, :], in_=x1t[:, :]),
                        inc=sX1[par], inc_k=16, waits=[v_n2])
            r_x1[par].set(v_st)

        FINALS = [sX1[0].total(), sX1[1].total()]
    return FINALS



def emit_p4(nc, b, D):
    RG = DM // NCORES
    RD = GH // NCORES
    gi_d = D["gidxT"]
    ui_d = D["uidxT"]
    di_d = D["didxT"]
    lut_d = D["lut"]
    wg_d = D["wgt"]
    wu_d = D["wut"]
    wd_d = D["wdt"]

    st = ExitStack()
    with st:
        lut_sb = st.enter_context(nc.sbuf_tensor("lut_sb", [128, 16], F32, allow_name_mangling=True))
        idx_sb = st.enter_context(nc.sbuf_tensor("idx_sb", [128, 2 * GH], F32, allow_name_mangling=True))
        acc_sb = st.enter_context(nc.sbuf_tensor("acc_sb", [128, 2 * GH], F32, allow_name_mangling=True))
        tmp_sb = st.enter_context(nc.sbuf_tensor("tmp_sb", [128, GH], F32, allow_name_mangling=True))
        out_sb = st.enter_context(nc.sbuf_tensor("out_sb", [128, 2 * GH], BF16, allow_name_mangling=True))

        sDVE, sPOOL = b.sem("sDVE"), b.sem("sPOOL")
        sIDX = b.sems(2, "sIDX")
        sOUT = b.sems(2, "sOUT")
        sLUT = b.sem("sLUT")

        r_idx = [Res(), Res()]
        r_acc = [Res(), Res()]
        r_tmp = Res()
        r_out = [Res(), Res()]

        ll = b.op("gpsimd", lambda g: g.dma_start(
            out=lut_sb[:, :], in_=lut_d.unsqueeze(0).partition_broadcast(128)),
            inc=sLUT, inc_k=16)

        jobs = []
        for (idx_d, w_d, R, C) in [(gi_d, wg_d, RG, GH), (ui_d, wu_d, RG, GH), (di_d, wd_d, RD, DM)]:
            r0 = 0
            while r0 < R:
                p = min(128, R - r0)
                jobs.append((idx_d, w_d, r0, p, C))
                r0 += 128
        ji = 0
        for (idx_d, w_d, r0, p, C) in jobs:
            par = ji & 1
            ji += 1
            il = b.op("gpsimd", lambda g, idx_d=idx_d, r0=r0, p=p, C=C, par=par: g.dma_start(
                out=idx_sb[0:p, par * GH:par * GH + C], in_=idx_d[r0:r0 + p, :]),
                inc=sIDX[par], inc_k=16, waits=[r_idx[par]])
            acc = acc_sb[:, par * GH:par * GH + C]
            vk = b.op("vector", lambda v, p=p, C=C, par=par, acc=acc: v.tensor_scalar(
                acc[0:p, :], idx_sb[0:p, par * GH:par * GH + C], 0.0, lut_sb[0:p, 0:1], ALU.is_equal, ALU.mult),
                inc=sDVE, waits=[il, ll, r_acc[par]])
            for k in range(1, 16):
                vt = b.op("vector", lambda v, p=p, C=C, k=k, par=par: v.tensor_scalar(
                    tmp_sb[0:p, 0:C], idx_sb[0:p, par * GH:par * GH + C], float(k), lut_sb[0:p, k:k + 1], ALU.is_equal, ALU.mult),
                    inc=sDVE, waits=[vk, r_tmp])
                vk = b.op("vector", lambda v, p=p, C=C, par=par, acc=acc: v.tensor_add(
                    acc[0:p, :], acc[0:p, :], tmp_sb[0:p, 0:C]),
                    inc=sDVE, waits=[vt])
                r_tmp.set(vk)
            r_idx[par].set(vk)
            vo = b.op("vector", lambda v, p=p, C=C, par=par, acc=acc: v.tensor_copy(
                out_sb[0:p, par * GH:par * GH + C], acc[0:p, :]),
                inc=sDVE, waits=[vk, r_out[par]])
            r_acc[par].set(vo)
            vs = b.op("gpsimd", lambda g, w_d=w_d, r0=r0, p=p, C=C, par=par: g.dma_start(
                out=w_d[r0:r0 + p, :], in_=out_sb[0:p, par * GH:par * GH + C]),
                inc=sOUT[par], inc_k=16, waits=[vo])
            r_out[par].set(vs)

        FINALS = [sOUT[0].total(), sOUT[1].total()]
    return FINALS



def emit_p5(nc, b, D):
    T = TPC
    TH = 1024
    NHF = T // TH
    NC = 512
    NO = GH // 128
    NM = DM // 128
    x1_d = D["x1"]
    wg_d = D["wg_f"]
    wu_d = D["wu_f"]
    wd_d = D["wd_f"]
    x2_d = D["x2"]
    xa_i = D["xa2_i"]

    st = ExitStack()
    with st:
        csts_sb = st.enter_context(nc.sbuf_tensor("csts_sb", [128, 8], F32, allow_name_mangling=True))
        ones1_sb = st.enter_context(nc.sbuf_tensor("ones1_sb", [1, 128], F32, allow_name_mangling=True))
        ones128_sb = st.enter_context(nc.sbuf_tensor("ones128_sb", [128, 1], F32, allow_name_mangling=True))
        sum_sb = st.enter_context(nc.sbuf_tensor("sum_sb", [1, TH], F32, allow_name_mangling=True))
        ssq_sb = st.enter_context(nc.sbuf_tensor("ssq_sb", [1, TH], F32, allow_name_mangling=True))
        tmp_sb = st.enter_context(nc.sbuf_tensor("tmp_sb", [1, TH], F32, allow_name_mangling=True))
        x1_sb = st.enter_context(nc.sbuf_tensor("x1_sb", [128, 16 * TH], BF16, allow_name_mangling=True))
        h_sb = st.enter_context(nc.sbuf_tensor("h_sb", [128, NO * TH], BF16, allow_name_mangling=True))
        wg_sb = st.enter_context(nc.sbuf_tensor("wg_sb", [128, 2 * 2048], BF16, allow_name_mangling=True))
        wu_sb = st.enter_context(nc.sbuf_tensor("wu_sb", [128, 2 * 2048], BF16, allow_name_mangling=True))
        wd_sb = st.enter_context(nc.sbuf_tensor("wd_sb", [128, 2 * 128], BF16, allow_name_mangling=True))
        sg_sb = st.enter_context(nc.sbuf_tensor("sg_sb", [128, TH], F32, allow_name_mangling=True))
        xa_sb = st.enter_context(nc.sbuf_tensor("xa_sb", [128, 2 * TH], F32, allow_name_mangling=True))
        sq_sb = st.enter_context(nc.sbuf_tensor("sq_sb", [128, 2 * TH], F32, allow_name_mangling=True))
        mubc_sb = st.enter_context(nc.sbuf_tensor("mubc_sb", [128, TH], F32, allow_name_mangling=True))
        rsbc_sb = st.enter_context(nc.sbuf_tensor("rsbc_sb", [128, TH], F32, allow_name_mangling=True))
        xab_sb = st.enter_context(nc.sbuf_tensor("xab_sb", [128, 2 * TH], F32, allow_name_mangling=True))
        x2_sb = st.enter_context(nc.sbuf_tensor("x2_sb", [128, 2 * TH], BF16, allow_name_mangling=True))

        pG = st.enter_context(nc.psum_tensor("pG", [128, TH], F32, allow_name_mangling=True))
        pU = st.enter_context(nc.psum_tensor("pU", [128, TH], F32, allow_name_mangling=True))
        pD = [st.enter_context(nc.psum_tensor(f"pD{i}", [128, TH], F32, allow_name_mangling=True)) for i in range(2)]

        sPE, sACT, sDVE, sPOOL = b.sem("sPE"), b.sem("sACT"), b.sem("sDVE"), b.sem("sPOOL")
        sX1 = b.sems(2, "sX1")
        sWGL = b.sems(2, "sWGL"); sWUL = b.sems(2, "sWUL"); sWDL = b.sems(2, "sWDL")
        sXA = b.sems(2, "sXA"); sXAL = b.sems(2, "sXAL"); sX2 = b.sems(2, "sX2")

        v_ones1 = b.op("gpsimd", lambda g: g.memset(ones1_sb[:, :], 1.0), inc=sPOOL)
        v_ones128 = b.op("gpsimd", lambda g: g.memset(ones128_sb[:, :], 1.0), inc=sPOOL)

        r_x1 = [Res(), Res()]
        r_wg = [Res(), Res()]; r_wu = [Res(), Res()]; r_wd = [Res(), Res()]
        r_pG = Res(); r_pU = Res(); r_pD = [Res(), Res()]
        r_sg = Res(); r_h = Res()
        r_xa = [Res(), Res()]; r_sq = [Res(), Res()]
        r_xab = [Res(), Res()]; r_x2 = [Res(), Res()]
        r_sum = Res(); r_ssq = Res()

        for hf in range(NHF):
            th0 = hf * TH
            for k in range(16):
                xl = b.op("sync", lambda s, k=k, th0=th0, kpar=k & 1: s.dma_start(
                    out=x1_sb[:, k * TH:(k + 1) * TH], in_=x1_d[k * 128:(k + 1) * 128, th0:th0 + TH]),
                    inc=sX1[0], inc_k=16, waits=[r_x1[0]] if k == 0 else [])
            v_x1 = sX1[0].total()
            v_zs = b.op("gpsimd", lambda g: g.memset(sum_sb[:, :], 0.0), inc=sPOOL, waits=[r_sum])
            v_zq = b.op("gpsimd", lambda g: g.memset(ssq_sb[:, :], 0.0), inc=sPOOL, waits=[r_ssq])

            v_h = None
            for o in range(NO):
                par = o & 1
                o0 = o * 128
                for k in range(16):
                    gl = b.op("sync", lambda g, k=k, o0=o0, par=par: g.dma_start(
                        out=wg_sb[:, par * 2048 + k * 128:par * 2048 + (k + 1) * 128],
                        in_=wg_d[k * 128:(k + 1) * 128, o0:o0 + 128]),
                        inc=sWGL[par], inc_k=16, waits=[r_wg[par]] if k == 0 else [])
                gl = sWGL[par].total()
                for k in range(16):
                    ul = b.op("sync", lambda g, k=k, o0=o0, par=par: g.dma_start(
                        out=wu_sb[:, par * 2048 + k * 128:par * 2048 + (k + 1) * 128],
                        in_=wu_d[k * 128:(k + 1) * 128, o0:o0 + 128]),
                        inc=sWUL[par], inc_k=16, waits=[r_wu[par]] if k == 0 else [])
                ul = sWUL[par].total()
                v_gm = None
                for j in range(TH // NC):
                    for k in range(16):
                        v_gm = b.op("tensor", lambda t, k=k, j=j, par=par: t.matmul(
                            pG[:, j * NC:(j + 1) * NC], wg_sb[:, par * 2048 + k * 128:par * 2048 + (k + 1) * 128],
                            x1_sb[:, k * TH + j * NC:k * TH + (j + 1) * NC],
                            start=(k == 0), stop=(k == 15)),
                            inc=sPE if k == 15 else None,
                            waits=([gl, (v_x1[0], v_x1[1]), r_pG] if j == 0 and k == 0 else []))
                v_um = None
                for j in range(TH // NC):
                    for k in range(16):
                        v_um = b.op("tensor", lambda t, k=k, j=j, par=par: t.matmul(
                            pU[:, j * NC:(j + 1) * NC], wu_sb[:, par * 2048 + k * 128:par * 2048 + (k + 1) * 128],
                            x1_sb[:, k * TH + j * NC:k * TH + (j + 1) * NC],
                            start=(k == 0), stop=(k == 15)),
                            inc=sPE if k == 15 else None,
                            waits=([ul, r_pU] if j == 0 and k == 0 else []))
                r_wg[par].set(v_gm)
                r_wu[par].set(v_um)
                v_sg = b.op("scalar", lambda s: s.activation(out=sg_sb[:, :], in_=pG[:, :], func=AF.Silu),
                            inc=sACT, waits=[v_gm, r_sg])
                r_pG.set(v_sg)
                v_h = b.op("vector", lambda v, o=o: v.tensor_mul(h_sb[:, o * TH:(o + 1) * TH], sg_sb[:, :], pU[:, :]),
                           inc=sDVE, waits=[v_sg, v_um, r_h])
                r_pU.set(v_h)
                r_sg.set(v_h)
            v_h_all = v_h
            v_a2 = None
            for m in range(NM):
                par = m & 1
                m0 = m * 128
                v_dm = None
                for o in range(NO):
                    wpar = o & 1
                    wl = b.op("sync", lambda g, o=o, m0=m0, wpar=wpar: g.dma_start(
                        out=wd_sb[:, wpar * 128:(wpar + 1) * 128], in_=wd_d[o * 128:(o + 1) * 128, m0:m0 + 128]),
                        inc=sWDL[wpar], inc_k=16, waits=[r_wd[wpar]])
                    for j in range(TH // NC):
                        v_dm = b.op("tensor", lambda t, o=o, j=j, par=par, wpar=wpar: t.matmul(
                            pD[par][:, j * NC:(j + 1) * NC], wd_sb[:, wpar * 128:(wpar + 1) * 128],
                            h_sb[:, o * TH + j * NC:o * TH + (j + 1) * NC],
                            start=(o == 0), stop=(o == NO - 1)),
                            inc=sPE if j == (TH // NC) - 1 else None,
                            waits=([wl] + ([v_h_all, r_pD[par]] if o == 0 else [])))
                    r_wd[wpar].set(v_dm)
                xa = xa_sb[:, par * TH:(par + 1) * TH]
                sq = sq_sb[:, par * TH:(par + 1) * TH]
                v_xa = b.op("vector", lambda v, m=m, xa=xa, par=par: v.tensor_add(
                    xa[:, :], pD[par][:, :], x1_sb[:, m * TH:(m + 1) * TH]),
                    inc=sDVE, waits=[v_dm, (v_x1[0], v_x1[1]), r_xa[par]])
                r_pD[par].set(v_xa)
                v_xas = b.op("gpsimd", lambda g, m0=m0, th0=th0, xa=xa: g.dma_start(
                    out=xa_i[m0:m0 + 128, th0:th0 + TH], in_=xa[:, :]),
                    inc=sXA[par], inc_k=16, waits=[v_xa])
                v_sq = b.op("scalar", lambda s, xa=xa, sq=sq: s.activation(out=sq[:, :], in_=xa[:, :], func=AF.Square),
                            inc=sACT, waits=[v_xa, r_sq[par]])
                r_xa[par].set(v_sq, v_xas)
                for j in range(TH // NC):
                    v_s1 = b.op("tensor", lambda t, j=j, xa=xa, par=par: t.matmul(
                        pD[par][0:1, j * NC:(j + 1) * NC], ones128_sb[:, 0:1], xa[:, j * NC:(j + 1) * NC], start=True, stop=True),
                        inc=sPE, waits=[v_xa, v_ones128, r_pD[par]])
                    v_a1 = b.op("vector", lambda v, j=j, par=par: v.tensor_add(
                        sum_sb[0:1, j * NC:(j + 1) * NC], sum_sb[0:1, j * NC:(j + 1) * NC], pD[par][0:1, j * NC:(j + 1) * NC]),
                        inc=sDVE, waits=[v_s1, v_zs])
                    v_s2 = b.op("tensor", lambda t, j=j, sq=sq, par=par: t.matmul(
                        pD[par][0:1, j * NC:(j + 1) * NC], ones128_sb[:, 0:1], sq[:, j * NC:(j + 1) * NC], start=True, stop=True),
                        inc=sPE, waits=[v_sq, v_a1])
                    v_a2 = b.op("vector", lambda v, j=j, par=par: v.tensor_add(
                        ssq_sb[0:1, j * NC:(j + 1) * NC], ssq_sb[0:1, j * NC:(j + 1) * NC], pD[par][0:1, j * NC:(j + 1) * NC]),
                        inc=sDVE, waits=[v_s2, v_zq])
                r_sq[par].set(v_s2)
                r_pD[par].set(v_a2)
            v_stats = v_a2
            r_x1[0].set(v_xa, v_gm, v_um)

            v_mu = b.op("vector", lambda v: v.tensor_scalar(sum_sb[0:1, :], sum_sb[0:1, :], 1.0 / DM, None, ALU.mult),
                        inc=sDVE, waits=[v_stats])
            v_m2 = b.op("vector", lambda v: v.tensor_mul(tmp_sb[0:1, :], sum_sb[0:1, :], sum_sb[0:1, :]),
                        inc=sDVE, waits=[v_mu])
            v_var = b.op("vector", lambda v: v.scalar_tensor_tensor(ssq_sb[0:1, :], ssq_sb[0:1, :], 1.0 / DM, tmp_sb[0:1, :], ALU.mult, ALU.subtract),
                         inc=sDVE, waits=[v_m2])
            v_ln = b.op("scalar", lambda s: s.activation(out=ssq_sb[0:1, :], in_=ssq_sb[0:1, :], func=AF.Ln, bias=1e-5),
                        inc=sACT, waits=[v_var])
            v_rs = b.op("scalar", lambda s: s.activation(out=ssq_sb[0:1, :], in_=ssq_sb[0:1, :], func=AF.Exp, scale=-0.5),
                        inc=sACT, waits=[v_ln])
            v_mbc = None
            for j in range(TH // NC):
                v_bm = b.op("tensor", lambda t, j=j: t.matmul(
                    pD[0][:, j * NC:(j + 1) * NC], ones1_sb[0:1, :], sum_sb[0:1, j * NC:(j + 1) * NC], start=True, stop=True),
                    inc=sPE, waits=[v_mu, v_ones1, r_pD[0]])
                v_mbc = b.op("vector", lambda v, j=j: v.tensor_copy(mubc_sb[:, j * NC:(j + 1) * NC], pD[0][:, j * NC:(j + 1) * NC]),
                             inc=sDVE, waits=[v_bm])
                r_pD[0].set(v_mbc)
            v_rbc = None
            for j in range(TH // NC):
                v_rm = b.op("tensor", lambda t, j=j: t.matmul(
                    pD[1][:, j * NC:(j + 1) * NC], ones1_sb[0:1, :], ssq_sb[0:1, j * NC:(j + 1) * NC], start=True, stop=True),
                    inc=sPE, waits=[v_rs, v_ones1, r_pD[1]])
                v_rbc = b.op("vector", lambda v, j=j: v.tensor_copy(rsbc_sb[:, j * NC:(j + 1) * NC], pD[1][:, j * NC:(j + 1) * NC]),
                             inc=sDVE, waits=[v_rm])
                r_pD[1].set(v_rbc)
            r_sum.set(v_mbc)
            r_ssq.set(v_rbc)
            for m in range(NM):
                par = m & 1
                m0 = m * 128
                al = b.op("gpsimd", lambda g, m0=m0, th0=th0, par=par: g.dma_start(
                    out=xab_sb[:, par * TH:(par + 1) * TH], in_=xa_i[m0:m0 + 128, th0:th0 + TH]),
                    inc=sXAL[par], inc_k=16, waits=[r_xab[par], sXA[0].total(), sXA[1].total()])
                xab = xab_sb[:, par * TH:(par + 1) * TH]
                x2t = x2_sb[:, par * TH:(par + 1) * TH]
                v_n1 = b.op("vector", lambda v, xab=xab: v.tensor_sub(xab[:, :], xab[:, :], mubc_sb[:, :]),
                            inc=sDVE, waits=[al, v_mbc])
                v_n2 = b.op("vector", lambda v, xab=xab, x2t=x2t: v.tensor_mul(x2t[:, :], xab[:, :], rsbc_sb[:, :]),
                            inc=sDVE, waits=[v_n1, v_rbc, r_x2[par]])
                r_xab[par].set(v_n2)
                v_st = b.op("gpsimd", lambda g, m0=m0, th0=th0, x2t=x2t: g.dma_start(
                    out=x2_d[m0:m0 + 128, th0:th0 + TH], in_=x2t[:, :]),
                    inc=sX2[par], inc_k=16, waits=[v_n2])
                r_x2[par].set(v_st)
            r_h.set(v_dm)

        FINALS = [sX2[0].total(), sX2[1].total()]
    return FINALS



def emit_p6a(nc, b, D):
    T = TPC
    NC = 512
    NCH = T // NC
    NM = DM // 128
    R = 8
    x2_d = D["x2"]
    g_d = D["gam"]
    be_d = D["bet"]
    wi_d = D["w_in2"]
    bi_d = D["b_in2"]
    xn_d = D["xn"]
    u_d = D["u"]

    st = ExitStack()
    with st:
        csts_sb = st.enter_context(nc.sbuf_tensor("csts_sb", [128, 8], F32, allow_name_mangling=True))
        ones1_sb = st.enter_context(nc.sbuf_tensor("ones1_sb", [1, 128], F32, allow_name_mangling=True))
        ones128_sb = st.enter_context(nc.sbuf_tensor("ones128_sb", [128, 1], F32, allow_name_mangling=True))
        sum_sb = st.enter_context(nc.sbuf_tensor("sum_sb", [1, T], F32, allow_name_mangling=True))
        ssq_sb = st.enter_context(nc.sbuf_tensor("ssq_sb", [1, T], F32, allow_name_mangling=True))
        tmp_sb = st.enter_context(nc.sbuf_tensor("tmp_sb", [1, T], F32, allow_name_mangling=True))
        x2_sb = st.enter_context(nc.sbuf_tensor("x2_sb", [128, 2 * T], F32, allow_name_mangling=True))
        sq_sb = st.enter_context(nc.sbuf_tensor("sq_sb", [128, 2 * T], F32, allow_name_mangling=True))
        mubc_sb = st.enter_context(nc.sbuf_tensor("mubc_sb", [128, T], F32, allow_name_mangling=True))
        rsbc_sb = st.enter_context(nc.sbuf_tensor("rsbc_sb", [128, T], F32, allow_name_mangling=True))
        gb_sb = st.enter_context(nc.sbuf_tensor("gb_sb", [128, 2], F32, allow_name_mangling=True))
        bb_sb = st.enter_context(nc.sbuf_tensor("bb_sb", [128, 2], F32, allow_name_mangling=True))
        wi_sb = st.enter_context(nc.sbuf_tensor("wi_sb", [128, 2 * R], BF16, allow_name_mangling=True))
        bi_sb = st.enter_context(nc.sbuf_tensor("bi_sb", [R, 1], F32, allow_name_mangling=True))
        xnf_sb = st.enter_context(nc.sbuf_tensor("xnf_sb", [128, 2 * T], F32, allow_name_mangling=True))
        xnb_sb = st.enter_context(nc.sbuf_tensor("xnb_sb", [128, 2 * T], BF16, allow_name_mangling=True))
        u_sb = st.enter_context(nc.sbuf_tensor("u_sb", [R, T], F32, allow_name_mangling=True))

        pS = st.enter_context(nc.psum_tensor("pS", [128, NC], F32, allow_name_mangling=True))
        pU = st.enter_context(nc.psum_tensor("pU", [R, T], F32, allow_name_mangling=True))

        sPE, sACT, sDVE, sPOOL = b.sem("sPE"), b.sem("sACT"), b.sem("sDVE"), b.sem("sPOOL")
        sX2 = b.sems(2, "sX2"); sX2b = b.sems(2, "sX2b")
        sGB = b.sems(2, "sGB"); sBB = b.sems(2, "sBB"); sWI = b.sems(2, "sWI"); sBI = b.sem("sBI")
        sXN = b.sems(2, "sXN"); sU = b.sem("sU")

        v_ones1 = b.op("gpsimd", lambda g: g.memset(ones1_sb[:, :], 1.0), inc=sPOOL)
        v_ones128 = b.op("gpsimd", lambda g: g.memset(ones128_sb[:, :], 1.0), inc=sPOOL)
        v_zs = b.op("gpsimd", lambda g: g.memset(sum_sb[:, :], 0.0), inc=sPOOL)
        v_zq = b.op("gpsimd", lambda g: g.memset(ssq_sb[:, :], 0.0), inc=sPOOL)

        r_x2 = [Res(), Res()]; r_sq = [Res(), Res()]; r_pS = Res(); r_pU = Res()
        r_gb = [Res(), Res()]; r_bb = [Res(), Res()]; r_wi = [Res(), Res()]
        r_xnf = [Res(), Res()]; r_xnb = [Res(), Res()]

        v_a2 = None
        for m in range(NM):
            par = m & 1
            m0 = m * 128
            xl = b.op("gpsimd", lambda g, m0=m0, par=par: g.dma_start(
                out=x2_sb[:, par * T:(par + 1) * T], in_=x2_d[m0:m0 + 128, :]),
                inc=sX2[par], inc_k=16, waits=[r_x2[par]])
            x2t = x2_sb[:, par * T:(par + 1) * T]
            sq = sq_sb[:, par * T:(par + 1) * T]
            v_sq = b.op("scalar", lambda s, x2t=x2t, sq=sq: s.activation(out=sq[:, :], in_=x2t[:, :], func=AF.Square),
                        inc=sACT, waits=[xl, r_sq[par]])
            for j in range(NCH):
                v_s1 = b.op("tensor", lambda t, j=j, x2t=x2t: t.matmul(
                    pS[0:1, 0:NC], ones128_sb[:, 0:1], x2t[:, j * NC:(j + 1) * NC], start=True, stop=True),
                    inc=sPE, waits=[xl, v_ones128, r_pS])
                v_a1 = b.op("vector", lambda v, j=j: v.tensor_add(sum_sb[0:1, j * NC:(j + 1) * NC], sum_sb[0:1, j * NC:(j + 1) * NC], pS[0:1, 0:NC]),
                            inc=sDVE, waits=[v_s1, v_zs])
                r_pS.set(v_a1)
                v_s2 = b.op("tensor", lambda t, j=j, sq=sq: t.matmul(
                    pS[0:1, 0:NC], ones128_sb[:, 0:1], sq[:, j * NC:(j + 1) * NC], start=True, stop=True),
                    inc=sPE, waits=[v_sq, v_a1, r_pS])
                v_a2 = b.op("vector", lambda v, j=j: v.tensor_add(ssq_sb[0:1, j * NC:(j + 1) * NC], ssq_sb[0:1, j * NC:(j + 1) * NC], pS[0:1, 0:NC]),
                            inc=sDVE, waits=[v_s2, v_zq])
                r_pS.set(v_a2)
            r_x2[par].set(v_s1, v_sq)
            r_sq[par].set(v_s2)
        v_stats = v_a2

        v_mu = b.op("vector", lambda v: v.tensor_scalar(sum_sb[0:1, :], sum_sb[0:1, :], 1.0 / DM, None, ALU.mult),
                    inc=sDVE, waits=[v_stats])
        v_m2 = b.op("vector", lambda v: v.tensor_mul(tmp_sb[0:1, :], sum_sb[0:1, :], sum_sb[0:1, :]),
                    inc=sDVE, waits=[v_mu])
        v_var = b.op("vector", lambda v: v.scalar_tensor_tensor(ssq_sb[0:1, :], ssq_sb[0:1, :], 1.0 / DM, tmp_sb[0:1, :], ALU.mult, ALU.subtract),
                     inc=sDVE, waits=[v_m2])
        v_ln = b.op("scalar", lambda s: s.activation(out=ssq_sb[0:1, :], in_=ssq_sb[0:1, :], func=AF.Ln, bias=1e-5),
                    inc=sACT, waits=[v_var])
        v_rs = b.op("scalar", lambda s: s.activation(out=ssq_sb[0:1, :], in_=ssq_sb[0:1, :], func=AF.Exp, scale=-0.5),
                    inc=sACT, waits=[v_ln])
        v_mbc = None
        for j in range(NCH):
            v_bm = b.op("tensor", lambda t, j=j: t.matmul(
                pS[:, 0:NC], ones1_sb[0:1, :], sum_sb[0:1, j * NC:(j + 1) * NC], start=True, stop=True),
                inc=sPE, waits=[v_mu, v_ones1, r_pS])
            v_mbc = b.op("vector", lambda v, j=j: v.tensor_copy(mubc_sb[:, j * NC:(j + 1) * NC], pS[:, 0:NC]),
                         inc=sDVE, waits=[v_bm])
            r_pS.set(v_mbc)
        v_rbc = None
        for j in range(NCH):
            v_rm = b.op("tensor", lambda t, j=j: t.matmul(
                pS[:, 0:NC], ones1_sb[0:1, :], ssq_sb[0:1, j * NC:(j + 1) * NC], start=True, stop=True),
                inc=sPE, waits=[v_rs, v_ones1, r_pS])
            v_rbc = b.op("vector", lambda v, j=j: v.tensor_copy(rsbc_sb[:, j * NC:(j + 1) * NC], pS[:, 0:NC]),
                         inc=sDVE, waits=[v_rm])
            r_pS.set(v_rbc)

        bil = b.op("sync", lambda s: s.dma_start(out=bi_sb[:, :], in_=col1(bi_d[:])), inc=sBI, inc_k=16)

        v_um = None
        for m in range(NM):
            par = m & 1
            m0 = m * 128
            xl = b.op("gpsimd", lambda g, m0=m0, par=par: g.dma_start(
                out=x2_sb[:, par * T:(par + 1) * T], in_=x2_d[m0:m0 + 128, :]),
                inc=sX2b[par], inc_k=16, waits=[r_x2[par]])
            gl = b.op("sync", lambda s, m0=m0, par=par: s.dma_start(
                out=gb_sb[:, par:par + 1], in_=col1(g_d[m0:m0 + 128])),
                inc=sGB[par], inc_k=16, waits=[r_gb[par]])
            bl = b.op("sync", lambda s, m0=m0, par=par: s.dma_start(
                out=bb_sb[:, par:par + 1], in_=col1(be_d[m0:m0 + 128])),
                inc=sBB[par], inc_k=16, waits=[r_bb[par]])
            wl = b.op("gpsimd", lambda g, m0=m0, par=par: g.dma_start(
                out=wi_sb[:, par * R:(par + 1) * R], in_=wi_d[m0:m0 + 128, :]),
                inc=sWI[par], inc_k=16, waits=[r_wi[par]])
            x2t = x2_sb[:, par * T:(par + 1) * T]
            xnf = xnf_sb[:, par * T:(par + 1) * T]
            xnb = xnb_sb[:, par * T:(par + 1) * T]
            v_n1 = b.op("vector", lambda v, x2t=x2t, xnf=xnf: v.tensor_sub(xnf[:, :], x2t[:, :], mubc_sb[:, :]),
                        inc=sDVE, waits=[xl, v_mbc, r_xnf[par]])
            r_x2[par].set(v_n1)
            v_n2 = b.op("vector", lambda v, xnf=xnf: v.tensor_mul(xnf[:, :], xnf[:, :], rsbc_sb[:, :]),
                        inc=sDVE, waits=[v_n1, v_rbc])
            v_n3 = b.op("vector", lambda v, xnf=xnf, xnb=xnb, par=par: v.tensor_scalar(
                xnb[:, :], xnf[:, :], gb_sb[:, par:par + 1], bb_sb[:, par:par + 1], ALU.mult, ALU.add),
                inc=sDVE, waits=[v_n2, gl, bl, r_xnb[par]])
            r_gb[par].set(v_n3)
            r_bb[par].set(v_n3)
            r_xnf[par].set(v_n3)
            v_xs = b.op("gpsimd", lambda g, m0=m0, xnb=xnb: g.dma_start(out=xn_d[m0:m0 + 128, :], in_=xnb[:, :]),
                        inc=sXN[par], inc_k=16, waits=[v_n3])
            for j in range(NCH):
                v_um = b.op("tensor", lambda t, m=m, j=j, par=par, xnb=xnb: t.matmul(
                    pU[:, j * NC:(j + 1) * NC], wi_sb[:, par * R:(par + 1) * R], xnb[:, j * NC:(j + 1) * NC],
                    start=(m == 0), stop=(m == NM - 1)),
                    inc=sPE if j == NCH - 1 else None,
                    waits=[v_n3, wl] + ([r_pU] if m == 0 and j == 0 else []))
            r_wi[par].set(v_um)
            r_xnb[par].set(v_um, v_xs)
        v_u = b.op("vector", lambda v: v.tensor_scalar(u_sb[:, :], pU[:, :], bi_sb[:, 0:1], None, ALU.add),
                   inc=sDVE, waits=[v_um, bil])
        r_pU.set(v_u)
        b.op("gpsimd", lambda g: g.dma_start(out=u_d[:, :], in_=u_sb[:, :]), inc=sU, inc_k=16, waits=[v_u])
        FINALS = [sU.total(), sXN[0].total(), sXN[1].total()]
    return FINALS



def emit_p6b(nc, b, D):
    T = TPC
    NC = 512
    NCH = T // NC
    NM = DM // 128
    R = 8
    RF = 32
    SEQ = 8192
    HOFF = SEQ - T
    ug = D["ug"]
    dlt_d = D["delta"]
    x2_d = D["x2"]
    xn_d = D["xn"]
    dl_d = D["dlog"]
    wo_d = D["w_out2"]
    bo_d = D["b_out2"]
    dsk_d = D["d_skip"]
    ga_d = D["gam"]
    be_d = D["bet"]
    wd_d = D["w_down"]
    bd_d = D["b_down"]
    wu_d = D["w_up"]
    bu_d = D["b_up"]
    x3_d = D["x3"]
    xt_i = D["xt_i"]
    tot_i = D["tot_i"]

    st = ExitStack()
    with st:
        csts_sb = st.enter_context(nc.sbuf_tensor("csts_sb", [128, 8], F32, allow_name_mangling=True))
        ones1_sb = st.enter_context(nc.sbuf_tensor("ones1_sb", [1, 128], F32, allow_name_mangling=True))
        ones128_sb = st.enter_context(nc.sbuf_tensor("ones128_sb", [128, 1], F32, allow_name_mangling=True))
        sum_sb = st.enter_context(nc.sbuf_tensor("sum_sb", [1, T], F32, allow_name_mangling=True))
        ssq_sb = st.enter_context(nc.sbuf_tensor("ssq_sb", [1, T], F32, allow_name_mangling=True))
        tmp_sb = st.enter_context(nc.sbuf_tensor("tmp_sb", [1, T], F32, allow_name_mangling=True))
        dec_sb = st.enter_context(nc.sbuf_tensor("dec_sb", [R, 1], F32, allow_name_mangling=True))
        dl8_sb = st.enter_context(nc.sbuf_tensor("dl8_sb", [R, 8], F32, allow_name_mangling=True))
        hown_sb = st.enter_context(nc.sbuf_tensor("hown_sb", [R, T], F32, allow_name_mangling=True))
        wo_sb = st.enter_context(nc.sbuf_tensor("wo_sb", [R, 2 * 128], F32, allow_name_mangling=True))
        bo_sb = st.enter_context(nc.sbuf_tensor("bo_sb", [128, 2], F32, allow_name_mangling=True))
        dsk_sb = st.enter_context(nc.sbuf_tensor("dsk_sb", [128, 2], F32, allow_name_mangling=True))
        gb2_sb = st.enter_context(nc.sbuf_tensor("gb2_sb", [128, 2], F32, allow_name_mangling=True))
        bb2_sb = st.enter_context(nc.sbuf_tensor("bb2_sb", [128, 2], F32, allow_name_mangling=True))
        wdn_sb = st.enter_context(nc.sbuf_tensor("wdn_sb", [128, 2 * RF], F32, allow_name_mangling=True))
        bdn_sb = st.enter_context(nc.sbuf_tensor("bdn_sb", [RF, 1], F32, allow_name_mangling=True))
        wup_sb = st.enter_context(nc.sbuf_tensor("wup_sb", [RF, 2 * 128], F32, allow_name_mangling=True))
        bup_sb = st.enter_context(nc.sbuf_tensor("bup_sb", [128, 2], F32, allow_name_mangling=True))

        stU = ExitStack()
        ha_sb = stU.enter_context(nc.sbuf_tensor("p6_ha", [R, SEQ], F32, allow_name_mangling=True))
        hb_sb = stU.enter_context(nc.sbuf_tensor("p6_hb", [R, SEQ], F32, allow_name_mangling=True))
        u_sb = stU.enter_context(nc.sbuf_tensor("p6_u", [R, 4 * T], F32, allow_name_mangling=True))
        dbc_sb = stU.enter_context(nc.sbuf_tensor("p6_dbc", [R, T], F32, allow_name_mangling=True))
        pA = st.enter_context(nc.psum_tensor("pA", [128, T], F32, allow_name_mangling=True))
        pF = st.enter_context(nc.psum_tensor("pF", [RF, T], F32, allow_name_mangling=True))

        sPE, sACT, sDVE, sPOOL = b.sem("sPE"), b.sem("sACT"), b.sem("sDVE"), b.sem("sPOOL")
        sULA = b.sems(2, "sULA"); sULB = b.sems(2, "sULB"); sMISC = b.sem("sMISC")
        sXN = b.sems(2, "sXN"); sX2 = b.sems(2, "sX2"); sXTL = b.sems(2, "sXTL")
        sWO = b.sems(2, "sWO"); sCOL = b.sems(2, "sCOL")
        sXTS = b.sems(2, "sXTS"); sTOT = b.sems(2, "sTOT"); sTOTL = b.sems(2, "sTOTL")
        sX3 = b.sems(2, "sX3")
        sWD = b.sems(2, "sWD"); sWU = b.sems(2, "sWU")

        v_ones1 = b.op("gpsimd", lambda g: g.memset(ones1_sb[:, :], 1.0), inc=sPOOL)
        v_ones128 = b.op("gpsimd", lambda g: g.memset(ones128_sb[:, :], 1.0), inc=sPOOL)
        v_zs = b.op("gpsimd", lambda g: g.memset(sum_sb[:, :], 0.0), inc=sPOOL)
        v_zq = b.op("gpsimd", lambda g: g.memset(ssq_sb[:, :], 0.0), inc=sPOOL)
        v_ones_d = b.op("gpsimd", lambda g: g.memset(dbc_sb[:, :], 1.0), inc=sPOOL)

        r_pA = Res(); r_pF = Res()
        r_u = [Res(), Res()]
        r_xn = [Res(), Res()]; r_x2 = [Res(), Res()]; r_xt = [Res(), Res()]; r_sq = [Res(), Res()]
        r_wo = [Res(), Res()]; r_cols = [Res(), Res()]
        r_wd = [Res(), Res()]; r_wu = [Res(), Res()]
        r_x3 = [Res(), Res()]

        dl = b.op("sync", lambda s: s.dma_start(out=dec_sb[:, :], in_=col1(dl_d[:])), inc=sMISC, inc_k=16)
        dll = b.op("gpsimd", lambda g: g.dma_start(out=dl8_sb[:, :], in_=dlt_d.unsqueeze(0).partition_broadcast(8)),
                   inc=b.sem("sDLT"), inc_k=16)
        v_dec = b.op("scalar", lambda s: s.activation(out=dec_sb[:, :], in_=dec_sb[:, :], func=AF.Sigmoid),
                     inc=sACT, waits=[dl])
        v_dbc = b.op("vector", lambda v: v.tensor_scalar(dbc_sb[:, :], dbc_sb[:, :], dec_sb[:, 0:1], None, ALU.mult),
                     inc=sDVE, waits=[v_dec, v_ones_d])
        v_ha = v_hb = None
        r_ua = [Res(), Res()]; r_ub = [Res(), Res()]
        for q in range(SEQ // T):
            ula = b.op("gpsimd", lambda g, q=q: g.dma_start(
                out=u_sb[:, (q & 1) * 2 * T:(q & 1) * 2 * T + T], in_=ug[8 * q:8 * q + 8, :]),
                inc=sULA[q & 1], inc_k=16, waits=[r_ua[q & 1]])
            ulb = b.op("gpsimd", lambda g, q=q: g.dma_start(
                out=u_sb[:, (q & 1) * 2 * T + T:(q & 1) * 2 * T + 2 * T], in_=ug[32 + 8 * q:32 + 8 * q + 8, :]),
                inc=sULB[q & 1], inc_k=16, waits=[r_ub[q & 1]])
            if q == 0:
                v_ha = b.op("vector", lambda v, q=q: v.tensor_tensor_scan(
                    ha_sb[:, q * T:(q + 1) * T], dbc_sb[:, :], u_sb[:, (q & 1) * 2 * T:(q & 1) * 2 * T + T], 0.0, ALU.mult, ALU.add),
                    inc=sDVE, waits=[v_dbc, ula])
                v_hb = b.op("vector", lambda v, q=q: v.tensor_tensor_scan(
                    hb_sb[:, q * T:(q + 1) * T], dbc_sb[:, :], u_sb[:, (q & 1) * 2 * T + T:(q & 1) * 2 * T + 2 * T], 0.0, ALU.mult, ALU.add),
                    inc=sDVE, waits=[v_dbc, ulb])
                r_ua[q & 1].set(v_ha)
                r_ub[q & 1].set(v_hb)
            else:
                v_ha = b.op("vector", lambda v, q=q: v.tensor_tensor_scan(
                    ha_sb[:, q * T:(q + 1) * T], dbc_sb[:, :], u_sb[:, (q & 1) * 2 * T:(q & 1) * 2 * T + T],
                    ha_sb[:, q * T - 1:q * T], ALU.mult, ALU.add),
                    inc=sDVE, waits=[v_ha, ula])
                v_hb = b.op("vector", lambda v, q=q: v.tensor_tensor_scan(
                    hb_sb[:, q * T:(q + 1) * T], dbc_sb[:, :], u_sb[:, (q & 1) * 2 * T + T:(q & 1) * 2 * T + 2 * T],
                    hb_sb[:, q * T - 1:q * T], ALU.mult, ALU.add),
                    inc=sDVE, waits=[v_hb, ulb])
            r_ua[q & 1].set(v_ha)
            r_ub[q & 1].set(v_hb)
        v_h = b.op("vector", lambda v: v.tensor_scalar(hown_sb[:, :], ha_sb[:, 0:T], dl8_sb[:, 0:1], None, ALU.mult),
                   inc=sDVE, waits=[v_ha, v_hb, dll])
        for k in range(1, 8):
            hsrc = ha_sb if k < 4 else hb_sb
            w = k % 4
            v_h = b.op("vector", lambda v, hsrc=hsrc, w=w, k=k: v.scalar_tensor_tensor(
                hown_sb[:, :], hsrc[:, w * T:(w + 1) * T], dl8_sb[:, k:k + 1], hown_sb[:, :], ALU.mult, ALU.add),
                inc=sDVE, waits=[v_h])
        b.op("gpsimd", lambda g: g.dma_start(out=D["hown_i"][:, :], in_=hown_sb[:, :]),
             inc=b.sem("sHOWN"), inc_k=16, waits=[v_h])
        stU.close()
        xn_sb = st.enter_context(nc.sbuf_tensor("xn_sb", [128, 2 * T], F32, allow_name_mangling=True))
        x2_sb = st.enter_context(nc.sbuf_tensor("x2_sb", [128, 2 * T], F32, allow_name_mangling=True))
        xt_sb = st.enter_context(nc.sbuf_tensor("xt_sb", [128, 2 * T], F32, allow_name_mangling=True))
        sq_sb = st.enter_context(nc.sbuf_tensor("sq_sb", [128, 2 * T], F32, allow_name_mangling=True))
        mubc_sb = st.enter_context(nc.sbuf_tensor("mubc_sb", [128, T], F32, allow_name_mangling=True))
        rsbc_sb = st.enter_context(nc.sbuf_tensor("rsbc_sb", [128, T], F32, allow_name_mangling=True))
        hf_sb = st.enter_context(nc.sbuf_tensor("hf_sb", [RF, T], F32, allow_name_mangling=True))
        x3_sb = st.enter_context(nc.sbuf_tensor("x3_sb", [128, 2 * T], BF16, allow_name_mangling=True))

        v_a2 = None
        for m in range(NM):
            par = m & 1
            m0 = m * 128
            wl = b.op("sync", lambda s, m0=m0, par=par: s.dma_start(
                out=wo_sb[:, par * 128:(par + 1) * 128], in_=wo_d[:, m0:m0 + 128]),
                inc=sWO[par], inc_k=16, waits=[r_wo[par]])
            cl1 = b.op("sync", lambda s, m0=m0, par=par: s.dma_start(
                out=bo_sb[:, par:par + 1], in_=col1(bo_d[m0:m0 + 128])),
                inc=sCOL[par], inc_k=16, waits=[r_cols[par]])
            cl2 = b.op("sync", lambda s, m0=m0, par=par: s.dma_start(
                out=dsk_sb[:, par:par + 1], in_=col1(dsk_d[m0:m0 + 128])),
                inc=sCOL[par], inc_k=16)
            xnl = b.op("gpsimd", lambda g, m0=m0, par=par: g.dma_start(
                out=xn_sb[:, par * T:(par + 1) * T], in_=xn_d[m0:m0 + 128, :]),
                inc=sXN[par], inc_k=16, waits=[r_xn[par]])
            x2l = b.op("gpsimd", lambda g, m0=m0, par=par: g.dma_start(
                out=x2_sb[:, par * T:(par + 1) * T], in_=x2_d[m0:m0 + 128, :]),
                inc=sX2[par], inc_k=16, waits=[r_x2[par]])
            v_mm = None
            for j in range(NCH):
                v_mm = b.op("tensor", lambda t, j=j, par=par: t.matmul(
                    pA[:, j * NC:(j + 1) * NC], wo_sb[:, par * 128:(par + 1) * 128],
                    hown_sb[:, j * NC:(j + 1) * NC], start=True, stop=True),
                    inc=sPE if j == NCH - 1 else None,
                    waits=[wl, v_h] + ([r_pA] if j == 0 else []))
            r_wo[par].set(v_mm)
            xn = xn_sb[:, par * T:(par + 1) * T]
            x2t = x2_sb[:, par * T:(par + 1) * T]
            xt = xt_sb[:, par * T:(par + 1) * T]
            sq = sq_sb[:, par * T:(par + 1) * T]
            v_e1 = b.op("vector", lambda v, xn=xn, xt=xt, par=par: v.scalar_tensor_tensor(
                xt[:, :], xn[:, :], dsk_sb[:, par:par + 1], pA[:, :], ALU.mult, ALU.add),
                inc=sDVE, waits=[v_mm, xnl, cl1, cl2, r_xt[par]])
            r_pA.set(v_e1)
            r_xn[par].set(v_e1)
            v_e2 = b.op("vector", lambda v, xt=xt, par=par: v.tensor_scalar(
                xt[:, :], xt[:, :], bo_sb[:, par:par + 1], None, ALU.add),
                inc=sDVE, waits=[v_e1])
            r_cols[par].set(v_e2)
            v_e3 = b.op("vector", lambda v, xt=xt, x2t=x2t: v.tensor_add(xt[:, :], xt[:, :], x2t[:, :]),
                        inc=sDVE, waits=[v_e2, x2l])
            r_x2[par].set(v_e3)
            v_xts = b.op("gpsimd", lambda g, m0=m0, xt=xt: g.dma_start(out=xt_i[m0:m0 + 128, :], in_=xt[:, :]),
                         inc=sXTS[par], inc_k=16, waits=[v_e3])
            v_sq = b.op("scalar", lambda s, xt=xt, sq=sq: s.activation(out=sq[:, :], in_=xt[:, :], func=AF.Square),
                        inc=sACT, waits=[v_e3, r_sq[par]])
            r_xt[par].set(v_sq, v_xts)
            for j in range(NCH):
                v_s1 = b.op("tensor", lambda t, j=j, xt=xt: t.matmul(
                    pA[0:1, j * NC:(j + 1) * NC], ones128_sb[:, 0:1], xt[:, j * NC:(j + 1) * NC], start=True, stop=True),
                    inc=sPE, waits=[v_e3, v_ones128, r_pA])
                v_a1 = b.op("vector", lambda v, j=j: v.tensor_add(sum_sb[0:1, j * NC:(j + 1) * NC], sum_sb[0:1, j * NC:(j + 1) * NC], pA[0:1, j * NC:(j + 1) * NC]),
                            inc=sDVE, waits=[v_s1, v_zs])
                v_s2 = b.op("tensor", lambda t, j=j, sq=sq: t.matmul(
                    pA[0:1, j * NC:(j + 1) * NC], ones128_sb[:, 0:1], sq[:, j * NC:(j + 1) * NC], start=True, stop=True),
                    inc=sPE, waits=[v_sq, v_a1])
                v_a2 = b.op("vector", lambda v, j=j: v.tensor_add(ssq_sb[0:1, j * NC:(j + 1) * NC], ssq_sb[0:1, j * NC:(j + 1) * NC], pA[0:1, j * NC:(j + 1) * NC]),
                            inc=sDVE, waits=[v_s2, v_zq])
                r_pA.set(v_a2)
            r_sq[par].set(v_s2)
        v_stats = v_a2

        def ln_finalize(prefix, v_stats_dep, prev_bc):
            v_mu = b.op("vector", lambda v: v.tensor_scalar(sum_sb[0:1, :], sum_sb[0:1, :], 1.0 / DM, None, ALU.mult),
                        inc=sDVE, waits=[v_stats_dep])
            v_m2 = b.op("vector", lambda v: v.tensor_mul(tmp_sb[0:1, :], sum_sb[0:1, :], sum_sb[0:1, :]),
                        inc=sDVE, waits=[v_mu])
            v_var = b.op("vector", lambda v: v.scalar_tensor_tensor(ssq_sb[0:1, :], ssq_sb[0:1, :], 1.0 / DM, tmp_sb[0:1, :], ALU.mult, ALU.subtract),
                         inc=sDVE, waits=[v_m2])
            v_ln = b.op("scalar", lambda s: s.activation(out=ssq_sb[0:1, :], in_=ssq_sb[0:1, :], func=AF.Ln, bias=1e-5),
                        inc=sACT, waits=[v_var])
            v_rs = b.op("scalar", lambda s: s.activation(out=ssq_sb[0:1, :], in_=ssq_sb[0:1, :], func=AF.Exp, scale=-0.5),
                        inc=sACT, waits=[v_ln])
            v_mbc = None
            for j in range(NCH):
                v_bm = b.op("tensor", lambda t, j=j: t.matmul(
                    pA[:, j * NC:(j + 1) * NC], ones1_sb[0:1, :], sum_sb[0:1, j * NC:(j + 1) * NC], start=True, stop=True),
                    inc=sPE, waits=[v_mu, v_ones1, r_pA])
                v_mbc = b.op("vector", lambda v, j=j: v.tensor_copy(mubc_sb[:, j * NC:(j + 1) * NC], pA[:, j * NC:(j + 1) * NC]),
                             inc=sDVE, waits=[v_bm] + prev_bc)
                r_pA.set(v_mbc)
            v_rbc = None
            for j in range(NCH):
                v_rm = b.op("tensor", lambda t, j=j: t.matmul(
                    pA[:, j * NC:(j + 1) * NC], ones1_sb[0:1, :], ssq_sb[0:1, j * NC:(j + 1) * NC], start=True, stop=True),
                    inc=sPE, waits=[v_rs, v_ones1, r_pA])
                v_rbc = b.op("vector", lambda v, j=j: v.tensor_copy(rsbc_sb[:, j * NC:(j + 1) * NC], pA[:, j * NC:(j + 1) * NC]),
                             inc=sDVE, waits=[v_rm] + prev_bc)
                r_pA.set(v_rbc)
            return v_mbc, v_rbc

        v_mbc, v_rbc = ln_finalize("ln1", v_stats, [])

        bdl = b.op("sync", lambda s: s.dma_start(out=bdn_sb[:, :], in_=col1(bd_d[:])), inc=sMISC, inc_k=16)

        v_fm = None
        for m in range(NM):
            par = m & 1
            m0 = m * 128
            xtl = b.op("gpsimd", lambda g, m0=m0, par=par: g.dma_start(
                out=xt_sb[:, par * T:(par + 1) * T], in_=xt_i[m0:m0 + 128, :]),
                inc=sXTL[par], inc_k=16, waits=[r_xt[par], sXTS[0].total(), sXTS[1].total()])
            gl = b.op("sync", lambda s, m0=m0, par=par: s.dma_start(
                out=gb2_sb[:, par:par + 1], in_=col1(ga_d[m0:m0 + 128])),
                inc=sCOL[par], inc_k=16, waits=[r_cols[par]])
            bl = b.op("sync", lambda s, m0=m0, par=par: s.dma_start(
                out=bb2_sb[:, par:par + 1], in_=col1(be_d[m0:m0 + 128])),
                inc=sCOL[par], inc_k=16)
            wl = b.op("sync", lambda s, m0=m0, par=par: s.dma_start(
                out=wdn_sb[:, par * RF:(par + 1) * RF], in_=wd_d[m0:m0 + 128, :]),
                inc=sWD[par], inc_k=16, waits=[r_wd[par]])
            xt = xt_sb[:, par * T:(par + 1) * T]
            sq = sq_sb[:, par * T:(par + 1) * T]
            v_n1 = b.op("vector", lambda v, xt=xt, sq=sq: v.tensor_sub(sq[:, :], xt[:, :], mubc_sb[:, :]),
                        inc=sDVE, waits=[xtl, v_mbc, r_sq[par]])
            v_n2 = b.op("vector", lambda v, sq=sq: v.tensor_mul(sq[:, :], sq[:, :], rsbc_sb[:, :]),
                        inc=sDVE, waits=[v_n1, v_rbc])
            v_n3 = b.op("vector", lambda v, sq=sq, par=par: v.tensor_scalar(
                sq[:, :], sq[:, :], gb2_sb[:, par:par + 1], bb2_sb[:, par:par + 1], ALU.mult, ALU.add),
                inc=sDVE, waits=[v_n2, gl, bl])
            r_cols[par].set(v_n3)
            r_xt[par].set(v_n3)
            for j in range(NCH):
                v_fm = b.op("tensor", lambda t, m=m, j=j, par=par, sq=sq: t.matmul(
                    pF[:, j * NC:(j + 1) * NC], wdn_sb[:, par * RF:(par + 1) * RF], sq[:, j * NC:(j + 1) * NC],
                    start=(m == 0), stop=(m == NM - 1)),
                    inc=sPE if j == NCH - 1 else None,
                    waits=[v_n3, wl] + ([r_pF] if m == 0 and j == 0 else []))
            r_wd[par].set(v_fm)
            r_sq[par].set(v_fm)
        v_hf = b.op("vector", lambda v: v.tensor_scalar(hf_sb[:, :], pF[:, :], bdn_sb[:, 0:1], 0.0, ALU.add, ALU.max),
                    inc=sDVE, waits=[v_fm, bdl])
        r_pF.set(v_hf)

        v_zs2 = b.op("gpsimd", lambda g: g.memset(sum_sb[:, :], 0.0), inc=sPOOL, waits=[v_mbc])
        v_zq2 = b.op("gpsimd", lambda g: g.memset(ssq_sb[:, :], 0.0), inc=sPOOL, waits=[v_rbc])
        v_a2b = None
        for m in range(NM):
            par = m & 1
            m0 = m * 128
            wl = b.op("sync", lambda s, m0=m0, par=par: s.dma_start(
                out=wup_sb[:, par * 128:(par + 1) * 128], in_=wu_d[:, m0:m0 + 128]),
                inc=sWU[par], inc_k=16, waits=[r_wu[par]])
            bl = b.op("sync", lambda s, m0=m0, par=par: s.dma_start(
                out=bup_sb[:, par:par + 1], in_=col1(bu_d[m0:m0 + 128])),
                inc=sCOL[par], inc_k=16, waits=[r_cols[par]])
            xtl = b.op("gpsimd", lambda g, m0=m0, par=par: g.dma_start(
                out=xt_sb[:, par * T:(par + 1) * T], in_=xt_i[m0:m0 + 128, :]),
                inc=sXTL[par], inc_k=16, waits=[r_xt[par]])
            x2l = b.op("gpsimd", lambda g, m0=m0, par=par: g.dma_start(
                out=x2_sb[:, par * T:(par + 1) * T], in_=x2_d[m0:m0 + 128, :]),
                inc=sX2[par], inc_k=16, waits=[r_x2[par]])
            v_mm = None
            for j in range(NCH):
                v_mm = b.op("tensor", lambda t, j=j, par=par: t.matmul(
                    pA[:, j * NC:(j + 1) * NC], wup_sb[:, par * 128:(par + 1) * 128],
                    hf_sb[:, j * NC:(j + 1) * NC], start=True, stop=True),
                    inc=sPE if j == NCH - 1 else None,
                    waits=[wl, v_hf] + ([r_pA] if j == 0 else []))
            r_wu[par].set(v_mm)
            xt = xt_sb[:, par * T:(par + 1) * T]
            x2t = x2_sb[:, par * T:(par + 1) * T]
            sq = sq_sb[:, par * T:(par + 1) * T]
            v_e1 = b.op("vector", lambda v, xt=xt: v.scalar_tensor_tensor(
                xt[:, :], xt[:, :], 1.0, pA[:, :], ALU.mult, ALU.add),
                inc=sDVE, waits=[v_mm, xtl, r_xt[par]])
            r_pA.set(v_e1)
            v_e2 = b.op("vector", lambda v, xt=xt, par=par: v.tensor_scalar(
                xt[:, :], xt[:, :], bup_sb[:, par:par + 1], None, ALU.add),
                inc=sDVE, waits=[v_e1, bl])
            r_cols[par].set(v_e2)
            v_e3 = b.op("vector", lambda v, xt=xt, x2t=x2t: v.tensor_add(xt[:, :], xt[:, :], x2t[:, :]),
                        inc=sDVE, waits=[v_e2, x2l])
            r_x2[par].set(v_e3)
            v_ts = b.op("gpsimd", lambda g, m0=m0, xt=xt: g.dma_start(out=tot_i[m0:m0 + 128, :], in_=xt[:, :]),
                        inc=sTOT[par], inc_k=16, waits=[v_e3])
            v_sq = b.op("scalar", lambda s, xt=xt, sq=sq: s.activation(out=sq[:, :], in_=xt[:, :], func=AF.Square),
                        inc=sACT, waits=[v_e3, r_sq[par]])
            r_xt[par].set(v_sq, v_ts)
            for j in range(NCH):
                v_s1 = b.op("tensor", lambda t, j=j, xt=xt: t.matmul(
                    pA[0:1, j * NC:(j + 1) * NC], ones128_sb[:, 0:1], xt[:, j * NC:(j + 1) * NC], start=True, stop=True),
                    inc=sPE, waits=[v_e3, v_ones128, r_pA])
                v_a1 = b.op("vector", lambda v, j=j: v.tensor_add(sum_sb[0:1, j * NC:(j + 1) * NC], sum_sb[0:1, j * NC:(j + 1) * NC], pA[0:1, j * NC:(j + 1) * NC]),
                            inc=sDVE, waits=[v_s1, v_zs2])
                v_s2 = b.op("tensor", lambda t, j=j, sq=sq: t.matmul(
                    pA[0:1, j * NC:(j + 1) * NC], ones128_sb[:, 0:1], sq[:, j * NC:(j + 1) * NC], start=True, stop=True),
                    inc=sPE, waits=[v_sq, v_a1])
                v_a2b = b.op("vector", lambda v, j=j: v.tensor_add(ssq_sb[0:1, j * NC:(j + 1) * NC], ssq_sb[0:1, j * NC:(j + 1) * NC], pA[0:1, j * NC:(j + 1) * NC]),
                            inc=sDVE, waits=[v_s2, v_zq2])
                r_pA.set(v_a2b)
            r_sq[par].set(v_s2)
        v_stats2 = v_a2b

        v_mbc2, v_rbc2 = ln_finalize("ln2", v_stats2, [v_mbc, v_rbc])
        for m in range(NM):
            par = m & 1
            m0 = m * 128
            tl = b.op("gpsimd", lambda g, m0=m0, par=par: g.dma_start(
                out=xt_sb[:, par * T:(par + 1) * T], in_=tot_i[m0:m0 + 128, :]),
                inc=sTOTL[par], inc_k=16, waits=[r_xt[par], sTOT[0].total(), sTOT[1].total()])
            xt = xt_sb[:, par * T:(par + 1) * T]
            x3t = x3_sb[:, par * T:(par + 1) * T]
            v_n1 = b.op("vector", lambda v, xt=xt: v.tensor_sub(xt[:, :], xt[:, :], mubc_sb[:, :]),
                        inc=sDVE, waits=[tl, v_mbc2])
            v_n2 = b.op("vector", lambda v, xt=xt, x3t=x3t: v.tensor_mul(x3t[:, :], xt[:, :], rsbc_sb[:, :]),
                        inc=sDVE, waits=[v_n1, v_rbc2, r_x3[par]])
            r_xt[par].set(v_n2)
            v_st = b.op("gpsimd", lambda g, m0=m0, x3t=x3t: g.dma_start(out=x3_d[m0:m0 + 128, :], in_=x3t[:, :]),
                        inc=sX3[par], inc_k=16, waits=[v_n2])
            r_x3[par].set(v_st)

        FINALS = [sX3[0].total(), sX3[1].total()]
    return FINALS



def emit_p7(nc, b, D):
    T = TPC
    NC = 512
    NCH = T // NC
    E = MOE_E
    NTT = T // 128
    x3_d = D["x3"]
    wg_d = D["wg8"]
    mp_d = D["maskprev"]
    tb_d = D["tbase"]
    flat_d = D["flat_o"]
    gk_d = D["gk_o"]
    inv_d = D["inv_o"]
    cnt_i = D["cnt_i"]
    cag_i = D["cag_i"]
    inv_i = D["inv_i"]

    st = ExitStack()
    with st:
        csts_sb = st.enter_context(nc.sbuf_tensor("csts_sb", [128, 8], F32, allow_name_mangling=True))
        ones1_sb = st.enter_context(nc.sbuf_tensor("ones1_sb", [1, 128], F32, allow_name_mangling=True))
        ones128_sb = st.enter_context(nc.sbuf_tensor("ones128_sb", [128, 1], F32, allow_name_mangling=True))
        io8_sb = st.enter_context(nc.sbuf_tensor("io8_sb", [128, 8], F32, allow_name_mangling=True))
        iorow_sb = st.enter_context(nc.sbuf_tensor("iorow_sb", [8, 8], F32, allow_name_mangling=True))
        iocol_sb = st.enter_context(nc.sbuf_tensor("iocol_sb", [8, 1], F32, allow_name_mangling=True))
        id8_sb = st.enter_context(nc.sbuf_tensor("id8_sb", [8, 8], F32, allow_name_mangling=True))
        io128r_sb = st.enter_context(nc.sbuf_tensor("io128r_sb", [128, 128], F32, allow_name_mangling=True))
        io128c_sb = st.enter_context(nc.sbuf_tensor("io128c_sb", [128, 1], F32, allow_name_mangling=True))
        u128_sb = st.enter_context(nc.sbuf_tensor("u128_sb", [128, 128], F32, allow_name_mangling=True))
        wg_sb = st.enter_context(nc.sbuf_tensor("wg_sb", [128, 16 * E], BF16, allow_name_mangling=True))
        x3_sb = st.enter_context(nc.sbuf_tensor("x3_sb", [128, 2 * T], BF16, allow_name_mangling=True))
        lg_sb = st.enter_context(nc.sbuf_tensor("lg_sb", [E, T], F32, allow_name_mangling=True))
        lgt_sb = st.enter_context(nc.sbuf_tensor("lgt_sb", [128, NTT * E], F32, allow_name_mangling=True))
        m8_sb = st.enter_context(nc.sbuf_tensor("m8_sb", [128, 8], F32, allow_name_mangling=True))
        mi_sb = st.enter_context(nc.sbuf_tensor("mi_sb", [128, 8], U32, allow_name_mangling=True))
        eid_sb = st.enter_context(nc.sbuf_tensor("eid_sb", [128, NTT], F32, allow_name_mangling=True))
        oh_sb = st.enter_context(nc.sbuf_tensor("oh_sb", [128, NTT * E], F32, allow_name_mangling=True))
        gate_sb = st.enter_context(nc.sbuf_tensor("gate_sb", [128, NTT], F32, allow_name_mangling=True))
        nl_sb = st.enter_context(nc.sbuf_tensor("nl_sb", [128, 8], F32, allow_name_mangling=True))
        ex_sb = st.enter_context(nc.sbuf_tensor("ex_sb", [128, 8], F32, allow_name_mangling=True))
        sume_sb = st.enter_context(nc.sbuf_tensor("sume_sb", [128, 1], F32, allow_name_mangling=True))
        cnt_sb = st.enter_context(nc.sbuf_tensor("cnt_sb", [1, E], F32, allow_name_mangling=True))
        mp_sb = st.enter_context(nc.sbuf_tensor("mp_sb", [E, 1], F32, allow_name_mangling=True))
        cag_sb = st.enter_context(nc.sbuf_tensor("cag_sb", [8, E], F32, allow_name_mangling=True))
        run_sb = st.enter_context(nc.sbuf_tensor("run_sb", [1, E], F32, allow_name_mangling=True))
        runbc_sb = st.enter_context(nc.sbuf_tensor("runbc_sb", [128, E], F32, allow_name_mangling=True))
        pos_sb = st.enter_context(nc.sbuf_tensor("pos_sb", [128, E], F32, allow_name_mangling=True))
        posc_sb = st.enter_context(nc.sbuf_tensor("posc_sb", [128, 1], F32, allow_name_mangling=True))
        keep_sb = st.enter_context(nc.sbuf_tensor("keep_sb", [128, 1], F32, allow_name_mangling=True))
        flat0_sb = st.enter_context(nc.sbuf_tensor("flat0_sb", [128, 1], F32, allow_name_mangling=True))
        flat_sb = st.enter_context(nc.sbuf_tensor("flat_sb", [128, NTT], F32, allow_name_mangling=True))
        flati_sb = st.enter_context(nc.sbuf_tensor("flati_sb", [128, NTT], I32, allow_name_mangling=True))
        big_sb = st.enter_context(nc.sbuf_tensor("big_sb", [128, 1], F32, allow_name_mangling=True))
        gk_sb = st.enter_context(nc.sbuf_tensor("gk_sb", [128, NTT], F32, allow_name_mangling=True))
        tb_sb = st.enter_context(nc.sbuf_tensor("tb_sb", [128, 1], F32, allow_name_mangling=True))
        tv_sb = st.enter_context(nc.sbuf_tensor("tv_sb", [128, NTT], F32, allow_name_mangling=True))
        z_sb = st.enter_context(nc.sbuf_tensor("z_sb", [128, 129], F32, allow_name_mangling=True))

        pL = st.enter_context(nc.psum_tensor("pL", [E, T], F32, allow_name_mangling=True))
        pT = st.enter_context(nc.psum_tensor("pT", [128, 128], F32, allow_name_mangling=True))
        pC = st.enter_context(nc.psum_tensor("pC", [128, 8], F32, allow_name_mangling=True))

        sPE, sACT, sDVE, sPOOL = b.sem("sPE"), b.sem("sACT"), b.sem("sDVE"), b.sem("sPOOL")
        sX3 = b.sems(2, "sX3"); sWG = b.sem("sWG"); sMISC = b.sem("sMISC")
        sZ = b.sem("sZ"); sCC = b.sem("sCC"); sSCT = b.sem("sSCT"); sOUT = b.sem("sOUT")

        r_pL = Res(); r_pT = Res(); r_pC = Res()
        r_x3 = [Res(), Res()]

        v_o1 = b.op("gpsimd", lambda g: g.memset(ones1_sb[:, :], 1.0), inc=sPOOL)
        v_o128 = b.op("gpsimd", lambda g: g.memset(ones128_sb[:, :], 1.0), inc=sPOOL)
        v_big = b.op("gpsimd", lambda g: g.memset(big_sb[:, :], float(MOE_T)), inc=sPOOL)
        v_i8 = b.op("gpsimd", lambda g: g.iota(io8_sb[:, :], pattern=[[1, 8]], channel_multiplier=0, allow_small_or_imprecise_dtypes=True), inc=sPOOL)
        v_ir = b.op("gpsimd", lambda g: g.iota(iorow_sb[:, :], pattern=[[1, 8]], channel_multiplier=0, allow_small_or_imprecise_dtypes=True), inc=sPOOL)
        v_ic = b.op("gpsimd", lambda g: g.iota(iocol_sb[:, :], pattern=[[0, 1]], channel_multiplier=1, allow_small_or_imprecise_dtypes=True), inc=sPOOL)
        v_id8 = b.op("vector", lambda v: v.tensor_scalar(id8_sb[:, :], iorow_sb[:, :], iocol_sb[:, 0:1], None, ALU.is_equal),
                     inc=sDVE, waits=[v_ir, v_ic])
        v_irr = b.op("gpsimd", lambda g: g.iota(io128r_sb[:, :], pattern=[[1, 128]], channel_multiplier=0, allow_small_or_imprecise_dtypes=True), inc=sPOOL)
        v_icc = b.op("gpsimd", lambda g: g.iota(io128c_sb[:, :], pattern=[[0, 1]], channel_multiplier=1, allow_small_or_imprecise_dtypes=True), inc=sPOOL)
        v_u128 = b.op("vector", lambda v: v.tensor_scalar(u128_sb[:, :], io128r_sb[:, :], io128c_sb[:, 0:1], None, ALU.is_ge),
                      inc=sDVE, waits=[v_irr, v_icc])
        v_zz = b.op("gpsimd", lambda g: g.memset(z_sb[:, :], 0.0), inc=sPOOL)

        wl = b.op("gpsimd", lambda g: g.dma_start(out=wg_sb[:, :].rearrange("p (k e) -> p k e", k=16),
                                                  in_=wg_d.rearrange("(k p) e -> p k e", p=128)),
                  inc=sWG, inc_k=16)
        ml = b.op("sync", lambda s: s.dma_start(out=mp_sb[:, :], in_=mp_d[:, :]), inc=sMISC, inc_k=16)
        tl = b.op("gpsimd", lambda g: g.dma_start(out=tb_sb[:, :], in_=tb_d.unsqueeze(0).partition_broadcast(128)),
                  inc=sMISC, inc_k=16)
        v_lg = None
        for j in range(NCH):
            for k in range(16):
                kpar = k & 1
                x3l = b.op("gpsimd", lambda g, k=k, j=j, kpar=kpar: g.dma_start(
                    out=x3_sb[:, kpar * T + j * NC:kpar * T + j * NC + NC], in_=x3_d[k * 128:(k + 1) * 128, j * NC:(j + 1) * NC]),
                    inc=sX3[kpar], inc_k=16, waits=[r_x3[kpar]])
                v_lg = b.op("tensor", lambda t, k=k, j=j, kpar=kpar: t.matmul(
                    pL[:, j * NC:(j + 1) * NC], wg_sb[:, k * E:(k + 1) * E],
                    x3_sb[:, kpar * T + j * NC:kpar * T + j * NC + NC],
                    start=(k == 0), stop=(k == 15)),
                    inc=sPE, waits=[x3l, wl] + ([r_pL] if k == 0 and j == 0 else []))
                r_x3[kpar].set(v_lg)
        v_lgc = b.op("vector", lambda v: v.tensor_copy(lg_sb[:, :], pL[:, :]), inc=sDVE, waits=[v_lg])
        r_pL.set(v_lgc)

        v_cnt = None
        r_ex = Res(); r_nl = Res()
        for tt in range(NTT):
            t0 = tt * 128
            v_tr = b.op("tensor", lambda t, t0=t0: t.transpose(pT[:, 0:8], lg_sb[:, t0:t0 + 128], id8_sb[:, :]),
                        inc=sPE, waits=[v_lgc, v_id8, r_pT])
            v_lt = b.op("vector", lambda v, tt=tt: v.tensor_copy(lgt_sb[:, tt * E:(tt + 1) * E], pT[:, 0:8]),
                        inc=sDVE, waits=[v_tr])
            r_pT.set(v_lt)
            v_mx = b.op("vector", lambda v, tt=tt: v.max(m8_sb[:, :], lgt_sb[:, tt * E:(tt + 1) * E]),
                        inc=sDVE, waits=[v_lt])
            v_mix = b.op("vector", lambda v, tt=tt: v.max_index(mi_sb[:, :], m8_sb[:, :], lgt_sb[:, tt * E:(tt + 1) * E]),
                         inc=sDVE, waits=[v_mx])
            v_eid = b.op("vector", lambda v, tt=tt: v.tensor_copy(eid_sb[:, tt:tt + 1], mi_sb[:, 0:1]),
                         inc=sDVE, waits=[v_mix])
            v_nl = b.op("vector", lambda v: v.tensor_scalar(nl_sb[:, 0:1], m8_sb[:, 0:1], -1.0, None, ALU.mult),
                        inc=sDVE, waits=[v_mx, r_nl])
            v_ex = b.op("scalar", lambda s, tt=tt: s.activation(out=ex_sb[:, :], in_=lgt_sb[:, tt * E:(tt + 1) * E], func=AF.Exp, bias=nl_sb[:, 0:1]),
                        inc=sACT, waits=[v_nl, v_lt, r_ex])
            r_nl.set(v_ex)
            v_se = b.op("vector", lambda v: v.reduce_sum(sume_sb[:, :], ex_sb[:, :], AX.X),
                        inc=sDVE, waits=[v_ex])
            r_ex.set(v_se)
            v_gt = b.op("vector", lambda v, tt=tt: v.reciprocal(gate_sb[:, tt:tt + 1], sume_sb[:, :]),
                        inc=sDVE, waits=[v_se])
            v_oh = b.op("vector", lambda v, tt=tt: v.tensor_scalar(
                oh_sb[:, tt * E:(tt + 1) * E], io8_sb[:, :], eid_sb[:, tt:tt + 1], None, ALU.is_equal),
                inc=sDVE, waits=[v_eid, v_i8])
            v_cnt = b.op("tensor", lambda t, tt=tt: t.matmul(
                pC[0:1, 0:8], ones128_sb[:, 0:1], oh_sb[:, tt * E:(tt + 1) * E],
                start=(tt == 0), stop=(tt == NTT - 1)),
                inc=sPE, waits=[v_oh, v_o128] + ([r_pC] if tt == 0 else []))
        v_cntc = b.op("vector", lambda v: v.tensor_copy(cnt_sb[:, :], pC[0:1, 0:8]), inc=sDVE, waits=[v_cnt])
        r_pC.set(v_cntc)
        v_cst = b.op("gpsimd", lambda g: g.dma_start(out=cnt_i[:, :], in_=cnt_sb[0:1, :].rearrange("one e -> e one") if False else cnt_sb[0:1, :]),
                     inc=sMISC, inc_k=16, waits=[v_cntc])
        v_ag = b.op("gpsimd", lambda g: g.collective_compute(
            "AllGather", ALU.bypass, replica_groups=[list(range(NCORES))],
            ins=[cnt_i.opt()], outs=[cag_i.opt()]),
            inc=sCC, waits=[v_cst])
        v_cagl = b.op("gpsimd", lambda g: g.dma_start(out=cag_sb[:, :], in_=cag_i.rearrange("(c e) one -> c (e one)", e=E)),
                      inc=sMISC, inc_k=16, waits=[v_ag])
        v_pt = b.op("tensor", lambda t: t.matmul(pC[0:1, 0:8], mp_sb[:, :], cag_sb[:, :], start=True, stop=True),
                    inc=sPE, waits=[v_cagl, ml, r_pC])
        v_run0 = b.op("vector", lambda v: v.tensor_copy(run_sb[:, :], pC[0:1, 0:8]), inc=sDVE, waits=[v_pt])
        r_pC.set(v_run0)

        b.op("gpsimd", lambda g: g.dma_start(out=inv_i[0:MOE_T, :].rearrange("(a b) one -> a (b one)", a=128), in_=z_sb[:, 0:128]),
             inc=sZ, inc_k=16, waits=[v_zz])
        b.op("gpsimd", lambda g: g.dma_start(out=inv_i[MOE_T:MOE_T + 1, :], in_=z_sb[0:1, 128:129]),
             inc=sZ, inc_k=16, waits=[v_zz])

        v_run = v_run0
        for tt in range(NTT):
            t0 = tt * 128
            v_rb = b.op("tensor", lambda t: t.matmul(pT[:, 0:8], ones1_sb[0:1, :], run_sb[0:1, :], start=True, stop=True),
                        inc=sPE, waits=[v_run, v_o1, r_pT])
            v_rbc = b.op("vector", lambda v: v.tensor_copy(runbc_sb[:, :], pT[:, 0:8]), inc=sDVE, waits=[v_rb])
            r_pT.set(v_rbc)
            v_cs = b.op("tensor", lambda t, tt=tt: t.matmul(pC[:, 0:8], u128_sb[:, :], oh_sb[:, tt * E:(tt + 1) * E], start=True, stop=True),
                        inc=sPE, waits=[v_u128, r_pC])
            v_p1 = b.op("vector", lambda v: v.tensor_add(pos_sb[:, :], pC[:, 0:8], runbc_sb[:, :]),
                        inc=sDVE, waits=[v_cs, v_rbc])
            r_pC.set(v_p1)
            v_p2 = b.op("vector", lambda v: v.tensor_scalar(pos_sb[:, :], pos_sb[:, :], -1.0, None, ALU.add),
                        inc=sDVE, waits=[v_p1])
            v_pm = b.op("vector", lambda v, tt=tt: v.tensor_mul(pos_sb[:, :], pos_sb[:, :], oh_sb[:, tt * E:(tt + 1) * E]),
                        inc=sDVE, waits=[v_p2])
            v_pc = b.op("vector", lambda v: v.reduce_sum(posc_sb[:, :], pos_sb[:, :], AX.X),
                        inc=sDVE, waits=[v_pm])
            v_tt = b.op("tensor", lambda t, tt=tt: t.matmul(pC[0:1, 0:8], ones128_sb[:, 0:1], oh_sb[:, tt * E:(tt + 1) * E], start=True, stop=True),
                        inc=sPE, waits=[v_pm, r_pC])
            v_run = b.op("vector", lambda v: v.tensor_add(run_sb[0:1, :], run_sb[0:1, :], pC[0:1, 0:8]),
                         inc=sDVE, waits=[v_tt, v_rb])
            r_pC.set(v_run)
            v_kp = b.op("vector", lambda v: v.tensor_scalar(keep_sb[:, :], posc_sb[:, :], float(MOE_C), None, ALU.is_lt),
                        inc=sDVE, waits=[v_pc])
            v_f0 = b.op("vector", lambda v, tt=tt: v.tensor_scalar(flat0_sb[:, :], eid_sb[:, tt:tt + 1], float(MOE_C), None, ALU.mult),
                        inc=sDVE, waits=[v_kp])
            v_f1 = b.op("vector", lambda v: v.tensor_add(flat0_sb[:, :], flat0_sb[:, :], posc_sb[:, :]),
                        inc=sDVE, waits=[v_f0])
            v_fa = b.op("vector", lambda v: v.tensor_scalar(flat0_sb[:, :], flat0_sb[:, :], -float(MOE_T), None, ALU.add),
                        inc=sDVE, waits=[v_f1])
            v_fb = b.op("vector", lambda v: v.tensor_mul(flat0_sb[:, :], flat0_sb[:, :], keep_sb[:, :]),
                        inc=sDVE, waits=[v_fa])
            v_fl = b.op("vector", lambda v, tt=tt: v.tensor_scalar(flat_sb[:, tt:tt + 1], flat0_sb[:, :], float(MOE_T), None, ALU.add),
                        inc=sDVE, waits=[v_fb])
            v_fi = b.op("vector", lambda v, tt=tt: v.tensor_copy(flati_sb[:, tt:tt + 1], flat_sb[:, tt:tt + 1]),
                        inc=sDVE, waits=[v_fl])
            v_gk = b.op("vector", lambda v, tt=tt: v.tensor_mul(gk_sb[:, tt:tt + 1], gate_sb[:, tt:tt + 1], keep_sb[:, :]),
                        inc=sDVE, waits=[v_kp, v_gt])
            v_tv = b.op("vector", lambda v, tt=tt, t0=t0: v.tensor_scalar(
                tv_sb[:, tt:tt + 1], io128c_sb[:, 0:1], float(t0 + 1), tb_sb[:, 0:1], ALU.add, ALU.add),
                inc=sDVE, waits=[v_icc, tl])
            v_sc = b.op("gpsimd", lambda g, tt=tt: g.indirect_dma_start(
                out=inv_i[:, :], out_offset=bass.IndirectOffsetOnAxis(flati_sb[:, tt:tt + 1], 0),
                in_=tv_sb[:, tt:tt + 1], in_offset=None),
                inc=sSCT, inc_k=16, waits=[v_fi, v_tv, sZ.total()])
        v_ar = b.op("gpsimd", lambda g: g.collective_compute(
            "ReduceScatter", ALU.max, replica_groups=[list(range(NCORES))],
            ins=[inv_i[0:MOE_T, :].opt()], outs=[inv_d[:, :].opt()]),
            inc=sCC, waits=[sSCT.total()])
        for tt in range(NTT):
            b.op("gpsimd", lambda g, tt=tt: g.dma_start(out=flat_d[tt * 128:(tt + 1) * 128, :], in_=flati_sb[:, tt:tt + 1]),
                 inc=sOUT, inc_k=16, waits=[v_fi])
            b.op("gpsimd", lambda g, tt=tt: g.dma_start(out=gk_d[tt * 128:(tt + 1) * 128, :], in_=gk_sb[:, tt:tt + 1]),
                 inc=sOUT, inc_k=16, waits=[v_gk])
        FINALS = [sOUT.total(), (sCC, sCC.n)]
    return FINALS



def emit_p8(nc, b, D):
    T = TPC
    NC = 512
    NCH = T // NC
    NM = DM // 128
    NO = MOE_H // 128
    NTT = T // 128
    x3r_d = D["x3r"]
    inv_d = D["inv_s"]
    w1_d = D["w1e"]
    w3_d = D["w3e"]
    w2_d = D["w2e"]
    x3_d = D["x3"]
    sw1_d = D["sw1"]
    sw3_d = D["sw3"]
    sw2_d = D["sw2"]
    oe_d = D["oe_r"]
    sh_d = D["sh_o"]

    st = ExitStack()
    stS = ExitStack()
    with st:
        ident_sb = st.enter_context(nc.sbuf_tensor("ident_sb", [128, 128], BF16, allow_name_mangling=True))
        ior_sb = st.enter_context(nc.sbuf_tensor("ior_sb", [128, 128], F32, allow_name_mangling=True))
        ioc_sb = st.enter_context(nc.sbuf_tensor("ioc_sb", [128, 1], F32, allow_name_mangling=True))
        h_sb = st.enter_context(nc.sbuf_tensor("h_sb", [128, NO * T], BF16, allow_name_mangling=True))
        wga_sb = st.enter_context(nc.sbuf_tensor("wga_sb", [128, 2 * 2048], BF16, allow_name_mangling=True))
        wgb_sb = st.enter_context(nc.sbuf_tensor("wgb_sb", [128, 2 * 2048], BF16, allow_name_mangling=True))
        wd_sb = st.enter_context(nc.sbuf_tensor("wd_sb", [128, 2 * 128], BF16, allow_name_mangling=True))
        sg_sb = st.enter_context(nc.sbuf_tensor("sg_sb", [128, T], F32, allow_name_mangling=True))
        ob_sb = st.enter_context(nc.sbuf_tensor("ob_sb", [128, 2 * T], BF16, allow_name_mangling=True))
        x3_sb = stS.enter_context(nc.sbuf_tensor("x3_sb", [128, 16 * T], BF16, allow_name_mangling=True))

        pG = st.enter_context(nc.psum_tensor("pG", [128, T], F32, allow_name_mangling=True))
        pU = st.enter_context(nc.psum_tensor("pU", [128, T], F32, allow_name_mangling=True))
        pT_ = pU[:, 0:64].bitcast(BF16)

        sPE, sACT, sDVE, sPOOL = b.sem("sPE"), b.sem("sACT"), b.sem("sDVE"), b.sem("sPOOL")
        sX3 = b.sem("sX3")
        sWA = b.sems(2, "sWA"); sWB = b.sems(2, "sWB"); sWD = b.sems(2, "sWD")
        sSH = b.sems(2, "sSH"); sOE = b.sems(2, "sOE")
        sIV = b.sem("sIV"); sGR = b.sems(2, "sGR")

        r_pG = Res(); r_pU = Res(); r_pT = r_pU
        r_wa = [Res(), Res()]; r_wb = [Res(), Res()]; r_wd = [Res(), Res()]
        r_sg = Res(); r_h = Res(); r_ob = [Res(), Res()]

        v_ir = b.op("gpsimd", lambda g: g.iota(ior_sb[:, :], pattern=[[1, 128]], channel_multiplier=0, allow_small_or_imprecise_dtypes=True), inc=sPOOL)
        v_ic = b.op("gpsimd", lambda g: g.iota(ioc_sb[:, :], pattern=[[0, 1]], channel_multiplier=1, allow_small_or_imprecise_dtypes=True), inc=sPOOL)
        v_id = b.op("vector", lambda v: v.tensor_scalar(ident_sb[:, :], ior_sb[:, :], ioc_sb[:, 0:1], None, ALU.is_equal),
                    inc=sDVE, waits=[v_ir, v_ic])

        def ffn(src_tiles_ap, src_dep, wg1_d, wg3_d, wdn_d, out_cb, tag):
            v_h = None
            for o in range(NO):
                par = o & 1
                o0 = o * 128
                for k in range(16):
                    ga = b.op("gpsimd", lambda g, k=k, o0=o0, par=par, wg1_d=wg1_d: g.dma_start(
                        out=wga_sb[:, par * 2048 + k * 128:par * 2048 + (k + 1) * 128],
                        in_=wg1_d[k * 128:(k + 1) * 128, o0:o0 + 128]),
                        inc=sWA[par], inc_k=16, waits=[r_wa[par]] if k == 0 else [])
                ga = sWA[par].total()
                for k in range(16):
                    gb = b.op("gpsimd", lambda g, k=k, o0=o0, par=par, wg3_d=wg3_d: g.dma_start(
                        out=wgb_sb[:, par * 2048 + k * 128:par * 2048 + (k + 1) * 128],
                        in_=wg3_d[k * 128:(k + 1) * 128, o0:o0 + 128]),
                        inc=sWB[par], inc_k=16, waits=[r_wb[par]] if k == 0 else [])
                gb = sWB[par].total()
                v_gm = None
                for j in range(NCH):
                    for k in range(16):
                        v_gm = b.op("tensor", lambda t, k=k, j=j, par=par: t.matmul(
                            pG[:, j * NC:(j + 1) * NC], wga_sb[:, par * 2048 + k * 128:par * 2048 + (k + 1) * 128],
                            src_tiles_ap(k)[:, j * NC:(j + 1) * NC], start=(k == 0), stop=(k == 15)),
                            inc=sPE if k == 15 else None,
                            waits=([ga, src_dep, r_pG] if j == 0 and k == 0 else []))
                v_um = None
                for j in range(NCH):
                    for k in range(16):
                        v_um = b.op("tensor", lambda t, k=k, j=j, par=par: t.matmul(
                            pU[:, j * NC:(j + 1) * NC], wgb_sb[:, par * 2048 + k * 128:par * 2048 + (k + 1) * 128],
                            src_tiles_ap(k)[:, j * NC:(j + 1) * NC], start=(k == 0), stop=(k == 15)),
                            inc=sPE if k == 15 else None,
                            waits=([gb, r_pU] if j == 0 and k == 0 else []))
                r_wa[par].set(v_gm)
                r_wb[par].set(v_um)
                v_sg = b.op("scalar", lambda s: s.activation(out=sg_sb[:, :], in_=pG[:, :], func=AF.Silu),
                            inc=sACT, waits=[v_gm, r_sg])
                r_pG.set(v_sg)
                v_h = b.op("vector", lambda v, o=o: v.tensor_mul(h_sb[:, o * T:(o + 1) * T], sg_sb[:, :], pU[:, :]),
                           inc=sDVE, waits=[v_sg, v_um, r_h])
                r_pU.set(v_h)
                r_sg.set(v_h)
            v_h_all = v_h
            v_dm_last = None
            for m in range(NM):
                par = m & 1
                m0 = m * 128
                v_dm = None
                for o in range(NO):
                    wpar = o & 1
                    wl = b.op("gpsimd", lambda g, o=o, m0=m0, wpar=wpar, wdn_d=wdn_d: g.dma_start(
                        out=wd_sb[:, wpar * 128:(wpar + 1) * 128], in_=wdn_d[o * 128:(o + 1) * 128, m0:m0 + 128]),
                        inc=sWD[wpar], inc_k=16, waits=[r_wd[wpar]])
                    for j in range(NCH):
                        v_dm = b.op("tensor", lambda t, o=o, j=j, wpar=wpar: t.matmul(
                            pG[:, j * NC:(j + 1) * NC], wd_sb[:, wpar * 128:(wpar + 1) * 128],
                            h_sb[:, o * T + j * NC:o * T + (j + 1) * NC],
                            start=(o == 0), stop=(o == NO - 1)),
                            inc=sPE if j == NCH - 1 else None,
                            waits=([wl] + ([v_h_all, r_pG] if o == 0 else [])))
                    r_wd[wpar].set(v_dm)
                out_cb(m, v_dm)
                v_dm_last = v_dm
            r_h.set(v_dm_last)
            return v_dm_last

        for k in range(16):
            b.op("gpsimd", lambda g, k=k: g.dma_start(out=x3_sb[:, k * T:(k + 1) * T], in_=x3_d[k * 128:(k + 1) * 128, :]),
                 inc=sX3, inc_k=16)
        v_x3 = sX3.total()

        def sh_out(m, v_dm):
            par = m & 1
            m0 = m * 128
            ob = ob_sb[:, par * T:(par + 1) * T]
            v_ev = b.op("vector", lambda v, ob=ob: v.tensor_copy(ob[:, :], pG[:, :]),
                        inc=sDVE, waits=[v_dm, r_ob[par]])
            r_pG.set(v_ev)
            v_st = b.op("gpsimd", lambda g, m0=m0, ob=ob: g.dma_start(out=sh_d[m0:m0 + 128, :], in_=ob[:, :]),
                        inc=sSH[par], inc_k=16, waits=[v_ev])
            r_ob[par].set(v_st)

        ffn(lambda k: x3_sb[:, k * T:(k + 1) * T], v_x3, sw1_d, sw3_d, sw2_d, sh_out, "sh")

        stS.close()
        BARS = [sX3.total(), sSH[0].total(), sSH[1].total(), (sPE, sPE.n), (sDVE, sDVE.n), (sACT, sACT.n)]
        iv_sb = st.enter_context(nc.sbuf_tensor("iv_sb", [128, NTT], F32, allow_name_mangling=True))
        ivq_sb = st.enter_context(nc.sbuf_tensor("ivq_sb", [128, NTT], F32, allow_name_mangling=True))
        ivi_sb = st.enter_context(nc.sbuf_tensor("ivi_sb", [128, NTT], I32, allow_name_mangling=True))
        xd_sb = st.enter_context(nc.sbuf_tensor("xd_sb", [128, 2 * DM], BF16, allow_name_mangling=True))
        xdT_sb = st.enter_context(nc.sbuf_tensor("xdT_sb", [128, 16 * T], BF16, allow_name_mangling=True))

        for tt in range(NTT):
            ivl = b.op("gpsimd", lambda g, tt=tt: g.dma_start(
                out=iv_sb[:, tt:tt + 1], in_=inv_d[tt * 128:(tt + 1) * 128, :]),
                inc=sIV, inc_k=16, waits=BARS if tt == 0 else [])
        ivl = sIV.total()
        v_ivz = b.op("vector", lambda v: v.tensor_scalar(xd_sb[0:128, 0:16].bitcast(F32) if False else iv_sb[:, :], iv_sb[:, :], 0.0, None, ALU.bypass),
                     inc=sDVE, waits=[ivl] + BARS)
        v_ivq = b.op("vector", lambda v: v.tensor_scalar(ivq_sb[:, :], iv_sb[:, :], 0.0, float(MOE_T + 1), ALU.is_equal, ALU.mult),
                     inc=sDVE, waits=[v_ivz])
        v_iv1 = b.op("vector", lambda v: v.tensor_scalar(iv_sb[:, :], iv_sb[:, :], -1.0, None, ALU.add),
                     inc=sDVE, waits=[v_ivq])
        v_iv1b = b.op("vector", lambda v: v.tensor_add(iv_sb[:, :], iv_sb[:, :], ivq_sb[:, :]),
                      inc=sDVE, waits=[v_iv1])
        v_iv2 = b.op("vector", lambda v: v.tensor_copy(ivi_sb[:, :], iv_sb[:, :]),
                     inc=sDVE, waits=[v_iv1b])
        v_zr = b.op("vector", lambda v: v.memset(xd_sb[0:1, 0:DM], 0.0), inc=sDVE, waits=BARS)
        v_zrs = b.op("gpsimd", lambda g: g.dma_start(out=x3r_d[MOE_T:MOE_T + 1, :], in_=xd_sb[0:1, 0:DM]),
                     inc=b.sem("sZR"), inc_k=16, waits=[v_zr])

        r_xd = [Res(), Res()]
        v_tr_last = None
        for tt in range(NTT):
            par = tt & 1
            t0 = tt * 128
            xd = xd_sb[:, par * DM:(par + 1) * DM]
            v_z = b.op("vector", lambda v, xd=xd: v.memset(xd[:, :], 0.0),
                       inc=sDVE, waits=[r_xd[par]] + (BARS if tt < 2 else []))
            v_g = b.op("gpsimd", lambda g, tt=tt, xd=xd: g.indirect_dma_start(
                out=xd[:, :], out_offset=None,
                in_=x3r_d[:, :], in_offset=bass.IndirectOffsetOnAxis(ivi_sb[:, tt:tt + 1], 0)),
                inc=sGR[par], inc_k=16, waits=[v_z, v_iv2, (b.sem("sZR"), b.sem("sZR").n)])
            for m in range(NM):
                v_tr = b.op("tensor", lambda t, m=m, xd=xd: t.transpose(
                    pT_[:, :], xd[:, m * 128:(m + 1) * 128], ident_sb[:, :]),
                    inc=sPE, waits=[v_g, v_id, r_pT])
                v_te = b.op("vector", lambda v, m=m, t0=t0: v.tensor_copy(
                    xdT_sb[:, m * T + t0:m * T + t0 + 128], pT_[:, :]),
                    inc=sDVE, waits=[v_tr] + (BARS if tt < 2 and m == 0 else []))
                r_pT.set(v_te)
                v_tr_last = v_te
            r_xd[par].set(v_tr)
        v_xdT = v_tr_last

        oeb_used = [Res(), Res()]

        def oe_out(m, v_dm):
            par = m & 1
            m0 = m * 128
            ob = ob_sb[:, par * T:(par + 1) * T]
            v_ev = b.op("vector", lambda v, ob=ob: v.tensor_copy(ob[:, :], pG[:, :]),
                        inc=sDVE, waits=[v_dm, r_ob[par]])
            r_pG.set(v_ev)
            v_st_last = None
            for tt in range(NTT):
                t0 = tt * 128
                v_tr = b.op("tensor", lambda t, t0=t0, ob=ob: t.transpose(
                    pT_[:, :], ob[:, t0:t0 + 128], ident_sb[:, :]),
                    inc=sPE, waits=[v_ev, r_pT])
                v_te = b.op("vector", lambda v, par=par: v.tensor_copy(xd_sb[:, par * DM:par * DM + 128], pT_[:, :]),
                            inc=sDVE, waits=[v_tr, oeb_used[par]])
                r_pT.set(v_te)
                v_st = b.op("gpsimd", lambda g, t0=t0, m0=m0, par=par: g.dma_start(
                    out=oe_d[t0:t0 + 128, m0:m0 + 128], in_=xd_sb[:, par * DM:par * DM + 128]),
                    inc=sOE[par], inc_k=16, waits=[v_te])
                oeb_used[par].set(v_st)
                v_st_last = v_st
            r_ob[par].set(v_ev, v_st_last)

        ffn(lambda k: xdT_sb[:, k * T:(k + 1) * T], v_xdT, w1_d, w3_d, w2_d, oe_out, "oe")

        FINALS = [sOE[0].total(), sOE[1].total(), sSH[0].total(), sSH[1].total()]
    return FINALS



def emit_p9(nc, b, D):
    T = TPC
    NC = 512
    NCH = T // NC
    NM = DM // 128
    NTT = T // 128
    oe_d = D["oe_f"]
    flat_d = D["flat_s"]
    gk_d = D["gk_s"]
    sh_d = D["sh_o"]
    x3_d = D["x3"]
    x4_d = D["x4"]
    tot_i = D["tot_i"]

    st = ExitStack()
    with st:
        csts_sb = st.enter_context(nc.sbuf_tensor("csts_sb", [128, 8], F32, allow_name_mangling=True))
        ones1_sb = st.enter_context(nc.sbuf_tensor("ones1_sb", [1, 128], F32, allow_name_mangling=True))
        ones128_sb = st.enter_context(nc.sbuf_tensor("ones128_sb", [128, 1], F32, allow_name_mangling=True))
        ior_sb = st.enter_context(nc.sbuf_tensor("ior_sb", [128, 128], F32, allow_name_mangling=True))
        ioc_sb = st.enter_context(nc.sbuf_tensor("ioc_sb", [128, 1], F32, allow_name_mangling=True))
        ident_sb = st.enter_context(nc.sbuf_tensor("ident_sb", [128, 128], BF16, allow_name_mangling=True))
        sum_sb = st.enter_context(nc.sbuf_tensor("sum_sb", [1, T], F32, allow_name_mangling=True))
        ssq_sb = st.enter_context(nc.sbuf_tensor("ssq_sb", [1, T], F32, allow_name_mangling=True))
        tmp_sb = st.enter_context(nc.sbuf_tensor("tmp_sb", [1, T], F32, allow_name_mangling=True))
        fl_sb = st.enter_context(nc.sbuf_tensor("fl_sb", [128, NTT], I32, allow_name_mangling=True))
        gkc_sb = st.enter_context(nc.sbuf_tensor("gkc_sb", [128, NTT], F32, allow_name_mangling=True))
        mo_sb = st.enter_context(nc.sbuf_tensor("mo_sb", [128, 2 * DM], BF16, allow_name_mangling=True))
        moT_sb = st.enter_context(nc.sbuf_tensor("moT_sb", [128, 16 * T], BF16, allow_name_mangling=True))
        mof_sb = st.enter_context(nc.sbuf_tensor("mof_sb", [128, 2 * T], F32, allow_name_mangling=True))
        sh_sb = st.enter_context(nc.sbuf_tensor("sh_sb", [128, 2 * T], F32, allow_name_mangling=True))
        x3_sb = st.enter_context(nc.sbuf_tensor("x3_sb", [128, 2 * T], F32, allow_name_mangling=True))
        sq_sb = st.enter_context(nc.sbuf_tensor("sq_sb", [128, 2 * T], F32, allow_name_mangling=True))
        mubc_sb = st.enter_context(nc.sbuf_tensor("mubc_sb", [128, T], F32, allow_name_mangling=True))
        rsbc_sb = st.enter_context(nc.sbuf_tensor("rsbc_sb", [128, T], F32, allow_name_mangling=True))
        x4_sb = st.enter_context(nc.sbuf_tensor("x4_sb", [128, 2 * T], F32, allow_name_mangling=True))

        pA = st.enter_context(nc.psum_tensor("pA", [128, T], F32, allow_name_mangling=True))
        pT_ = st.enter_context(nc.psum_tensor("pT_", [128, 128], BF16, allow_name_mangling=True))

        sPE, sACT, sDVE, sPOOL = b.sem("sPE"), b.sem("sACT"), b.sem("sDVE"), b.sem("sPOOL")
        sFL = b.sem("sFL"); sGK = b.sem("sGK")
        sGR = b.sems(2, "sGR"); sSH = b.sems(2, "sSH"); sX3 = b.sems(2, "sX3")
        sTOT = b.sems(2, "sTOT"); sTOTL = b.sems(2, "sTOTL"); sX4 = b.sems(2, "sX4")

        v_o1 = b.op("gpsimd", lambda g: g.memset(ones1_sb[:, :], 1.0), inc=sPOOL)
        v_o128 = b.op("gpsimd", lambda g: g.memset(ones128_sb[:, :], 1.0), inc=sPOOL)
        v_zs = b.op("gpsimd", lambda g: g.memset(sum_sb[:, :], 0.0), inc=sPOOL)
        v_zq = b.op("gpsimd", lambda g: g.memset(ssq_sb[:, :], 0.0), inc=sPOOL)
        v_ir = b.op("gpsimd", lambda g: g.iota(ior_sb[:, :], pattern=[[1, 128]], channel_multiplier=0, allow_small_or_imprecise_dtypes=True), inc=sPOOL)
        v_ic = b.op("gpsimd", lambda g: g.iota(ioc_sb[:, :], pattern=[[0, 1]], channel_multiplier=1, allow_small_or_imprecise_dtypes=True), inc=sPOOL)
        v_id = b.op("vector", lambda v: v.tensor_scalar(ident_sb[:, :], ior_sb[:, :], ioc_sb[:, 0:1], None, ALU.is_equal),
                    inc=sDVE, waits=[v_ir, v_ic])

        r_pA = Res(); r_pT = Res()
        r_mo = [Res(), Res()]
        r_mof = [Res(), Res()]; r_sh = [Res(), Res()]; r_x3 = [Res(), Res()]; r_sq = [Res(), Res()]
        r_x4 = [Res(), Res()]; r_xt = [Res(), Res()]

        for tt in range(NTT):
            b.op("gpsimd", lambda g, tt=tt: g.dma_start(out=fl_sb[:, tt:tt + 1], in_=flat_d[tt * 128:(tt + 1) * 128, :]),
                 inc=sFL, inc_k=16)
            b.op("gpsimd", lambda g, tt=tt: g.dma_start(out=gkc_sb[:, tt:tt + 1], in_=gk_d[tt * 128:(tt + 1) * 128, :]),
                 inc=sGK, inc_k=16)
        v_fl = sFL.total()
        v_gk = sGK.total()
        v_zr = b.op("vector", lambda v: v.memset(mo_sb[0:1, 0:DM], 0.0), inc=sDVE)
        v_zrs = b.op("gpsimd", lambda g: g.dma_start(out=oe_d[MOE_T:MOE_T + 1, :], in_=mo_sb[0:1, 0:DM]),
                     inc=b.sem("sZR9"), inc_k=16, waits=[v_zr])

        v_te_last = None
        for tt in range(NTT):
            par = tt & 1
            t0 = tt * 128
            mo = mo_sb[:, par * DM:(par + 1) * DM]
            v_z = b.op("vector", lambda v, mo=mo: v.memset(mo[:, :], 0.0),
                       inc=sDVE, waits=[r_mo[par]])
            v_g = b.op("gpsimd", lambda g, tt=tt, mo=mo: g.indirect_dma_start(
                out=mo[:, :], out_offset=None,
                in_=oe_d[:, :], in_offset=bass.IndirectOffsetOnAxis(fl_sb[:, tt:tt + 1], 0)),
                inc=sGR[par], inc_k=16, waits=[v_z, v_fl, (b.sem("sZR9"), b.sem("sZR9").n)])
            v_gm = b.op("vector", lambda v, tt=tt, mo=mo: v.tensor_scalar(
                mo[:, :], mo[:, :], gkc_sb[:, tt:tt + 1], None, ALU.mult),
                inc=sDVE, waits=[v_g, v_gk])
            for m in range(NM):
                v_tr = b.op("tensor", lambda t, m=m, mo=mo: t.transpose(
                    pT_[:, :], mo[:, m * 128:(m + 1) * 128], ident_sb[:, :]),
                    inc=sPE, waits=[v_gm, v_id, r_pT])
                v_te = b.op("vector", lambda v, m=m, t0=t0: v.tensor_copy(
                    moT_sb[:, m * T + t0:m * T + t0 + 128], pT_[:, :]),
                    inc=sDVE, waits=[v_tr])
                r_pT.set(v_te)
                v_te_last = v_te
            r_mo[par].set(v_tr)
        v_moT = v_te_last

        v_a2 = None
        for m in range(NM):
            par = m & 1
            m0 = m * 128
            shl = b.op("gpsimd", lambda g, m0=m0, par=par: g.dma_start(
                out=sh_sb[:, par * T:(par + 1) * T], in_=sh_d[m0:m0 + 128, :]),
                inc=sSH[par], inc_k=16, waits=[r_sh[par]])
            x3l = b.op("gpsimd", lambda g, m0=m0, par=par: g.dma_start(
                out=x3_sb[:, par * T:(par + 1) * T], in_=x3_d[m0:m0 + 128, :]),
                inc=sX3[par], inc_k=16, waits=[r_x3[par]])
            mof = mof_sb[:, par * T:(par + 1) * T]
            sh = sh_sb[:, par * T:(par + 1) * T]
            x3t = x3_sb[:, par * T:(par + 1) * T]
            sq = sq_sb[:, par * T:(par + 1) * T]
            v_c1 = b.op("vector", lambda v, m=m, mof=mof: v.tensor_copy(mof[:, :], moT_sb[:, m * T:(m + 1) * T]),
                        inc=sDVE, waits=[v_moT, r_mof[par]])
            v_c2 = b.op("vector", lambda v, mof=mof, sh=sh: v.tensor_add(mof[:, :], mof[:, :], sh[:, :]),
                        inc=sDVE, waits=[v_c1, shl])
            r_sh[par].set(v_c2)
            v_c3 = b.op("vector", lambda v, mof=mof, x3t=x3t: v.tensor_add(mof[:, :], mof[:, :], x3t[:, :]),
                        inc=sDVE, waits=[v_c2, x3l])
            r_x3[par].set(v_c3)
            v_ts = b.op("gpsimd", lambda g, m0=m0, mof=mof: g.dma_start(out=tot_i[m0:m0 + 128, :], in_=mof[:, :]),
                        inc=sTOT[par], inc_k=16, waits=[v_c3])
            v_sq = b.op("scalar", lambda s, mof=mof, sq=sq: s.activation(out=sq[:, :], in_=mof[:, :], func=AF.Square),
                        inc=sACT, waits=[v_c3, r_sq[par]])
            r_mof[par].set(v_sq, v_ts)
            for j in range(NCH):
                v_s1 = b.op("tensor", lambda t, j=j, mof=mof: t.matmul(
                    pA[0:1, j * NC:(j + 1) * NC], ones128_sb[:, 0:1], mof[:, j * NC:(j + 1) * NC], start=True, stop=True),
                    inc=sPE, waits=[v_c3, v_o128, r_pA])
                v_a1 = b.op("vector", lambda v, j=j: v.tensor_add(sum_sb[0:1, j * NC:(j + 1) * NC], sum_sb[0:1, j * NC:(j + 1) * NC], pA[0:1, j * NC:(j + 1) * NC]),
                            inc=sDVE, waits=[v_s1, v_zs])
                v_s2 = b.op("tensor", lambda t, j=j, sq=sq: t.matmul(
                    pA[0:1, j * NC:(j + 1) * NC], ones128_sb[:, 0:1], sq[:, j * NC:(j + 1) * NC], start=True, stop=True),
                    inc=sPE, waits=[v_sq, v_a1])
                v_a2 = b.op("vector", lambda v, j=j: v.tensor_add(ssq_sb[0:1, j * NC:(j + 1) * NC], ssq_sb[0:1, j * NC:(j + 1) * NC], pA[0:1, j * NC:(j + 1) * NC]),
                            inc=sDVE, waits=[v_s2, v_zq])
                r_pA.set(v_a2)
            r_sq[par].set(v_s2)
        v_stats = v_a2

        v_mu = b.op("vector", lambda v: v.tensor_scalar(sum_sb[0:1, :], sum_sb[0:1, :], 1.0 / DM, None, ALU.mult),
                    inc=sDVE, waits=[v_stats])
        v_m2 = b.op("vector", lambda v: v.tensor_mul(tmp_sb[0:1, :], sum_sb[0:1, :], sum_sb[0:1, :]),
                    inc=sDVE, waits=[v_mu])
        v_var = b.op("vector", lambda v: v.scalar_tensor_tensor(ssq_sb[0:1, :], ssq_sb[0:1, :], 1.0 / DM, tmp_sb[0:1, :], ALU.mult, ALU.subtract),
                     inc=sDVE, waits=[v_m2])
        v_ln = b.op("scalar", lambda s: s.activation(out=ssq_sb[0:1, :], in_=ssq_sb[0:1, :], func=AF.Ln, bias=1e-5),
                    inc=sACT, waits=[v_var])
        v_rs = b.op("scalar", lambda s: s.activation(out=ssq_sb[0:1, :], in_=ssq_sb[0:1, :], func=AF.Exp, scale=-0.5),
                    inc=sACT, waits=[v_ln])
        v_mbc = None
        for j in range(NCH):
            v_bm = b.op("tensor", lambda t, j=j: t.matmul(
                pA[:, j * NC:(j + 1) * NC], ones1_sb[0:1, :], sum_sb[0:1, j * NC:(j + 1) * NC], start=True, stop=True),
                inc=sPE, waits=[v_mu, v_o1, r_pA])
            v_mbc = b.op("vector", lambda v, j=j: v.tensor_copy(mubc_sb[:, j * NC:(j + 1) * NC], pA[:, j * NC:(j + 1) * NC]),
                         inc=sDVE, waits=[v_bm])
            r_pA.set(v_mbc)
        v_rbc = None
        for j in range(NCH):
            v_rm = b.op("tensor", lambda t, j=j: t.matmul(
                pA[:, j * NC:(j + 1) * NC], ones1_sb[0:1, :], ssq_sb[0:1, j * NC:(j + 1) * NC], start=True, stop=True),
                inc=sPE, waits=[v_rs, v_o1, r_pA])
            v_rbc = b.op("vector", lambda v, j=j: v.tensor_copy(rsbc_sb[:, j * NC:(j + 1) * NC], pA[:, j * NC:(j + 1) * NC]),
                         inc=sDVE, waits=[v_rm])
            r_pA.set(v_rbc)
        for m in range(NM):
            par = m & 1
            m0 = m * 128
            tl = b.op("gpsimd", lambda g, m0=m0, par=par: g.dma_start(
                out=x4_sb[:, par * T:(par + 1) * T], in_=tot_i[m0:m0 + 128, :]),
                inc=sTOTL[par], inc_k=16, waits=[r_xt[par], sTOT[0].total(), sTOT[1].total()])
            x4 = x4_sb[:, par * T:(par + 1) * T]
            v_n1 = b.op("vector", lambda v, x4=x4: v.tensor_sub(x4[:, :], x4[:, :], mubc_sb[:, :]),
                        inc=sDVE, waits=[tl, v_mbc])
            v_n2 = b.op("vector", lambda v, x4=x4: v.tensor_mul(x4[:, :], x4[:, :], rsbc_sb[:, :]),
                        inc=sDVE, waits=[v_n1, v_rbc])
            v_st = b.op("gpsimd", lambda g, m0=m0, x4=x4: g.dma_start(out=x4_d[m0:m0 + 128, :], in_=x4[:, :]),
                        inc=sX4[par], inc_k=16, waits=[v_n2])
            r_xt[par].set(v_st, v_n2)

        FINALS = [sX4[0].total(), sX4[1].total()]
    return FINALS



def emit_x3rows(nc, b, D):
    x3_d = D["x3"]
    xr_d = D["x3r"]
    T = TPC
    NM = DM // 128
    NTT = T // 128
    st = ExitStack()
    with st:
        ior = st.enter_context(nc.sbuf_tensor("xr_ior", [128, 128], F32, allow_name_mangling=True))
        ioc = st.enter_context(nc.sbuf_tensor("xr_ioc", [128, 1], F32, allow_name_mangling=True))
        ident = st.enter_context(nc.sbuf_tensor("xr_id", [128, 128], BF16, allow_name_mangling=True))
        xm_sb = st.enter_context(nc.sbuf_tensor("xr_xm", [128, 2 * T], BF16, allow_name_mangling=True))
        stg_sb = st.enter_context(nc.sbuf_tensor("xr_stg", [128, 2 * 128], BF16, allow_name_mangling=True))
        pT_ = st.enter_context(nc.psum_tensor("xr_pT", [128, 128], BF16, allow_name_mangling=True))
        sPE, sDVE, sPOOL = b.sem("sPE"), b.sem("sDVE"), b.sem("sPOOL")
        sXM = b.sems(2, "sXM"); sXR = b.sems(2, "sXR")
        r_xm = [Res(), Res()]; r_stg = [Res(), Res()]; r_pT = Res()
        v_ir = b.op("gpsimd", lambda g: g.iota(ior[:, :], pattern=[[1, 128]], channel_multiplier=0, allow_small_or_imprecise_dtypes=True), inc=sPOOL)
        v_ic = b.op("gpsimd", lambda g: g.iota(ioc[:, :], pattern=[[0, 1]], channel_multiplier=1, allow_small_or_imprecise_dtypes=True), inc=sPOOL)
        v_id = b.op("vector", lambda v: v.tensor_scalar(ident[:, :], ior[:, :], ioc[:, 0:1], None, ALU.is_equal),
                    inc=sDVE, waits=[v_ir, v_ic])
        for m in range(NM):
            par = m & 1
            m0 = m * 128
            xl = b.op("gpsimd", lambda g, m0=m0, par=par: g.dma_start(
                out=xm_sb[:, par * T:(par + 1) * T], in_=x3_d[m0:m0 + 128, :]),
                inc=sXM[par], inc_k=16, waits=[r_xm[par]])
            v_tr = None
            for tt in range(NTT):
                spar = tt & 1
                t0 = tt * 128
                v_tr = b.op("tensor", lambda t, t0=t0, par=par: t.transpose(
                    pT_[:, :], xm_sb[:, par * T + t0:par * T + t0 + 128], ident[:, :]),
                    inc=sPE, waits=[xl, v_id, r_pT])
                v_te = b.op("vector", lambda v, spar=spar: v.tensor_copy(stg_sb[:, spar * 128:(spar + 1) * 128], pT_[:, :]),
                            inc=sDVE, waits=[v_tr, r_stg[spar]])
                r_pT.set(v_te)
                v_st = b.op("gpsimd", lambda g, t0=t0, m0=m0, spar=spar: g.dma_start(
                    out=xr_d[t0:t0 + 128, m0:m0 + 128], in_=stg_sb[:, spar * 128:(spar + 1) * 128]),
                    inc=sXR[spar], inc_k=16, waits=[v_te])
                r_stg[spar].set(v_st)
            r_xm[par].set(v_tr)
        FINALS = [sXR[0].total(), sXR[1].total()]
    return FINALS



def build_full(debug=False):
    nc = bass.Bass("TRN2", target_bir_lowering=False)
    T = TPC
    F, Bf, I = F32, BF16, I32
    dram = {}

    def din(name, shape, dt):
        dram[name] = nc.dram_tensor(name, shape, dt, kind="ExternalInput").ap()

    def dint(name, shape, dt):
        dram[name] = nc.dram_tensor(name, shape, dt, kind="Internal").ap()

    def dsh(name, shape, dt):
        dram[name] = nc.dram_tensor(name, shape, dt, kind="Internal", addr_space="Shared").ap()

    din("xT", [DM, T + 3], F); din("w_in", [DM, 2 * DIN], F)
    din("conv_w", [DIN, 4], F); din("conv_b", [DIN], F)
    din("w_xp", [DIN, DTR + 2 * NST], F); din("w_dt", [DTR, DIN], F); din("b_dt", [DIN], F)
    din("acols", [128, 64], F); din("cid", [1], F)
    din("w_out", [DIN, DM], F); din("d_par", [DIN], F)
    din("gidxT", [DM // NCORES, GH], I); din("uidxT", [DM // NCORES, GH], I)
    din("didxT", [GH // NCORES, DM], I); din("lut", [16], F)
    din("gam", [DM], F); din("bet", [DM], F)
    din("w_in2", [DM, 8], F); din("b_in2", [8], F)
    din("delta", [8], F); din("dlog", [8], F)
    din("w_out2", [8, DM], F); din("b_out2", [DM], F); din("d_skip", [DM], F)
    din("w_down", [DM, 32], F); din("b_down", [32], F)
    din("w_up", [32, DM], F); din("b_up", [DM], F)
    din("wg8", [DM, MOE_E], F); din("maskprev", [MOE_E, 1], F); din("tbase", [1], F)
    din("w1e", [DM, MOE_H], F); din("w3e", [DM, MOE_H], F); din("w2e", [MOE_H, DM], F)
    din("sw1", [DM, MOE_H], F); din("sw3", [DM, MOE_H], F); din("sw2", [MOE_H, DM], F)

    dint("hs_i", [DIN, T], Bf); dint("gate_i", [DIN, T], Bf)
    dint("dt_i", [DIN, T], F); dint("dtu_i", [DIN, T], Bf)
    dint("bn_i", [NST, T], F); dint("cn_i", [NST, T], F)
    dsh("dtg", [NCORES * DIN, T], F); dsh("dtug", [NCORES * DIN, T], Bf)
    dsh("bng", [NCORES * NST, T], F); dsh("cng", [NCORES * NST, T], F)
    dint("y_i", [DIN, T], Bf); dsh("yg", [NCORES * DIN, T], Bf)
    dint("p3xa", [DM, T], F); dint("x1_i", [DM, T], Bf)
    dint("wg_sh", [DM // NCORES, GH], Bf); dint("wu_sh", [DM // NCORES, GH], Bf)
    dint("wd_sh", [GH // NCORES, DM], Bf)
    dsh("wgg", [DM, GH], Bf); dsh("wug", [DM, GH], Bf); dsh("wdg", [GH, DM], Bf)
    dint("xa2_i", [DM, T], F); dint("x2_i", [DM, T], Bf)
    dint("xn_i", [DM, T], Bf); dint("u_i", [8, T], F); dint("ug", [NCORES * 8, T], F)
    dint("p6xt", [DM, T], F); dint("p6tot", [DM, T], F); dint("x3_i", [DM, T], Bf)
    dint("hown_i", [8, T], F)
    dint("cnt_i", [MOE_E, 1], F); dint("cag_i", [NCORES * MOE_E, 1], F)
    dint("inv_i", [MOE_T + 1, 1], F); dint("inv_s", [MOE_C, 1], F)
    dint("flat_i", [T, 1], I); dint("gk_i", [T, 1], F)
    dint("x3r_i", [T, DM], Bf); dsh("x3rg", [MOE_T + 1, DM], Bf)
    dint("oer_i", [T, DM], Bf); dsh("oeg", [MOE_T + 1, DM], Bf)
    dint("sh_i", [DM, T], Bf); dint("tot2_i", [DM, T], F)
    x4 = nc.dram_tensor("x4", [DM, T], F, kind="ExternalOutput").ap()
    dram["x4"] = x4

    st = ExitStack()
    with st:
        csts_sb = st.enter_context(nc.sbuf_tensor("g_csts", [128, 8], F32, allow_name_mangling=True))
        block = st.enter_context(nc.Block())
        b = B(nc, st)
        sPOOL = b.sem("sPOOL")
        cd = setup_consts(nc, csts_sb, b, sPOOL, [1e-6, 1.0 / DTR, 1.0 / NST, -0.5, 1e-5, 1.0 / DM])

        def ag(pairs, deps, kind="AllGather", op=ALU.bypass):
            sCC = b.sem("sCC")
            vs = list(deps)
            out = []
            for (s_, d_) in pairs:
                out.append(b.op("gpsimd", lambda g, s_=s_, d_=d_: g.collective_compute(
                    kind, op, replica_groups=[list(range(NCORES))],
                    ins=[s_.opt()], outs=[d_.opt()]), inc=sCC, waits=vs))
            return out

        D = dram
        f = emit_p4(nc, b, {"gidxT": D["gidxT"], "uidxT": D["uidxT"], "didxT": D["didxT"],
                            "lut": D["lut"], "wgt": D["wg_sh"], "wut": D["wu_sh"], "wdt": D["wd_sh"]})
        b.barrier(f)
        f = ag([(D["wg_sh"], D["wgg"]), (D["wu_sh"], D["wug"]), (D["wd_sh"], D["wdg"])], [])
        b.barrier(f)
        f = emit_p1(nc, b, {"xT": D["xT"], "w_in": D["w_in"], "conv_w": D["conv_w"],
                            "conv_b": D["conv_b"], "w_xp": D["w_xp"], "w_dt": D["w_dt"],
                            "b_dt": D["b_dt"], "hs": D["hs_i"], "gate": D["gate_i"],
                            "dt": D["dt_i"], "dtu": D["dtu_i"], "bn": D["bn_i"], "cn": D["cn_i"]})
        b.barrier(f)
        f = ag([(D["dt_i"], D["dtg"]), (D["dtu_i"], D["dtug"]),
                (D["bn_i"], D["bng"]), (D["cn_i"], D["cng"])], [])
        b.barrier(f)
        f = emit_p2(nc, b, {"dtg": D["dtg"], "dtug": D["dtug"], "bng": D["bng"], "cng": D["cng"],
                            "acols": D["acols"], "cid": D["cid"], "y_i": D["y_i"]})
        b.barrier(f)
        f = ag([(D["y_i"], D["yg"])], [])
        b.barrier(f)
        f = emit_p3(nc, b, {"yg": D["yg"], "cid": D["cid"], "hs": D["hs_i"], "gate": D["gate_i"],
                            "xT": D["xT"][:, 3:], "w_out": D["w_out"], "d_par": D["d_par"],
                            "x1": D["x1_i"], "xa_i": D["p3xa"]})
        b.barrier(f)
        f = emit_p5(nc, b, {"x1": D["x1_i"], "wg_f": D["wgg"], "wu_f": D["wug"], "wd_f": D["wdg"],
                            "x2": D["x2_i"], "xa2_i": D["xa2_i"]})
        b.barrier(f)
        f = emit_p6a(nc, b, {"x2": D["x2_i"], "gam": D["gam"], "bet": D["bet"],
                             "w_in2": D["w_in2"], "b_in2": D["b_in2"],
                             "xn": D["xn_i"], "u": D["u_i"]})
        b.barrier(f)
        f = ag([(D["u_i"], D["ug"])], [])
        b.barrier(f)
        f = emit_p6b(nc, b, {"ug": D["ug"], "delta": D["delta"], "x2": D["x2_i"], "xn": D["xn_i"],
                             "dlog": D["dlog"], "w_out2": D["w_out2"], "b_out2": D["b_out2"],
                             "d_skip": D["d_skip"], "gam": D["gam"], "bet": D["bet"],
                             "w_down": D["w_down"], "b_down": D["b_down"],
                             "w_up": D["w_up"], "b_up": D["b_up"],
                             "x3": D["x3_i"], "xt_i": D["p6xt"], "tot_i": D["p6tot"],
                             "hown_i": D["hown_i"]})
        b.barrier(f)
        f = emit_p7(nc, b, {"x3": D["x3_i"], "wg8": D["wg8"], "maskprev": D["maskprev"],
                            "tbase": D["tbase"], "flat_o": D["flat_i"], "gk_o": D["gk_i"],
                            "inv_o": D["inv_s"], "cnt_i": D["cnt_i"], "cag_i": D["cag_i"],
                            "inv_i": D["inv_i"]})
        b.barrier(f)
        f = emit_x3rows(nc, b, {"x3": D["x3_i"], "x3r": D["x3r_i"]})
        b.barrier(f)
        f = ag([(D["x3r_i"], D["x3rg"][0:MOE_T, :])], [])
        b.barrier(f)
        f = emit_p8(nc, b, {"x3r": D["x3rg"], "inv_s": D["inv_s"],
                            "w1e": D["w1e"], "w3e": D["w3e"], "w2e": D["w2e"],
                            "x3": D["x3_i"], "sw1": D["sw1"], "sw3": D["sw3"], "sw2": D["sw2"],
                            "oe_r": D["oer_i"], "sh_o": D["sh_i"]})
        b.barrier(f)
        f = ag([(D["oer_i"], D["oeg"][0:MOE_T, :])], [])
        b.barrier(f)
        f = emit_p9(nc, b, {"oe_f": D["oeg"], "flat_s": D["flat_i"], "gk_s": D["gk_i"],
                            "sh_o": D["sh_i"], "x3": D["x3_i"], "x4": D["x4"],
                            "tot_i": D["tot2_i"]})
        if debug:
            b.barrier(f)
            sDBG = b.sem("sDBG")
            dbg_names = ["hs_i", "dt_i", "bn_i", "y_i", "x1_i", "wgg", "x2_i", "u_i", "ug",
                         "xn_i", "p6xt", "p6tot", "hown_i",
                         "x3_i", "flat_i", "gk_i", "inv_s", "oer_i", "sh_i", "x3rg", "oeg"]
            for nm in dbg_names:
                t_ = D[nm]
                o_ = nc.dram_tensor("dbg_" + nm, list(t_.shape), t_.dtype, kind="ExternalOutput").ap()
                f = [b.op("gpsimd", lambda g, t_=t_, o_=o_: g.dma_start(out=o_, in_=t_),
                          inc=sDBG, inc_k=16)]
            f = [sDBG.total()]
        b.op("gpsimd", lambda g: g.engine_nop(), waits=f)
        b.finalize(block)
    return nc



def kernel(**ins):
    x = np.asarray(ins["L0_attn_x"])
    xtb = [np.ascontiguousarray(x[0].T), np.ascontiguousarray(x[1].T)]
    conv_w = np.ascontiguousarray(np.asarray(ins["L0_attn_conv_w"]).reshape(DIN, 4))
    gidxT = np.ascontiguousarray(np.asarray(ins["L0_ffn_gate_idx"]).T).astype(np.int32)
    uidxT = np.ascontiguousarray(np.asarray(ins["L0_ffn_up_idx"]).T).astype(np.int32)
    didxT = np.ascontiguousarray(np.asarray(ins["L0_ffn_down_idx"]).T).astype(np.int32)
    alog = np.asarray(ins["L0_attn_A_log"]).astype(np.float32)
    RG = DM // NCORES
    RD = GH // NCORES

    maps = []
    for i in range(NCORES):
        bsel, s0 = i // 4, (i % 4) * TPC
        ext = np.zeros((DM, TPC + 3), np.float32)
        if s0 == 0:
            ext[:, 3:] = xtb[bsel][:, 0:TPC]
        else:
            ext[:, :] = xtb[bsel][:, s0 - 3:s0 + TPC]
        acols = np.ascontiguousarray(alog[512 * i:512 * (i + 1)].reshape(64, 128).T)
        mp = np.zeros((MOE_E, 1), np.float32)
        mp[:i] = 1.0
        delta = np.zeros(8, np.float32)
        delta[i] = 1.0
        maps.append({
            "xT": ext, "w_in": ins["L0_attn_in_proj_w"], "conv_w": conv_w,
            "conv_b": ins["L0_attn_conv_b"], "w_xp": ins["L0_attn_x_proj_w"],
            "w_dt": ins["L0_attn_dt_proj_w"], "b_dt": ins["L0_attn_dt_proj_b"],
            "acols": acols, "cid": np.array([i], np.float32),
            "w_out": ins["L0_attn_out_proj_w"], "d_par": ins["L0_attn_D_param"],
            "gidxT": np.ascontiguousarray(gidxT[i * RG:(i + 1) * RG]),
            "uidxT": np.ascontiguousarray(uidxT[i * RG:(i + 1) * RG]),
            "didxT": np.ascontiguousarray(didxT[i * RD:(i + 1) * RD]),
            "lut": ins["L0_ffn_lut"],
            "gam": ins["L1_attn_ln_gamma"], "bet": ins["L1_attn_ln_beta"],
            "w_in2": ins["L1_attn_W_in"], "b_in2": ins["L1_attn_b_in"],
            "delta": delta, "dlog": ins["L1_attn_decay_logit"],
            "w_out2": ins["L1_attn_W_out"], "b_out2": ins["L1_attn_b_out"],
            "d_skip": ins["L1_attn_D_skip"],
            "w_down": ins["L1_attn_W_down"], "b_down": ins["L1_attn_b_down"],
            "w_up": ins["L1_attn_W_up"], "b_up": ins["L1_attn_b_up"],
            "wg8": ins["L1_ffn_Wg"], "maskprev": mp, "tbase": np.array([i * TPC], np.float32),
            "w1e": np.asarray(ins["L1_ffn_W1"][i]), "w3e": np.asarray(ins["L1_ffn_W3"][i]),
            "w2e": np.asarray(ins["L1_ffn_W2"][i]),
            "sw1": ins["L1_ffn_sw1"], "sw3": ins["L1_ffn_sw3"], "sw2": ins["L1_ffn_sw2"],
        })
    nc = build_full()
    res = run_bass_kernel_spmd(nc, maps, core_ids=list(range(NCORES))).results
    out = np.empty((2, 8192, DM), np.float32)
    for i in range(NCORES):
        bsel, s0 = i // 4, (i % 4) * TPC
        out[bsel, s0:s0 + TPC, :] = np.asarray(res[i]["x4"]).T
    return out
